# version 27
# baseline (speedup 1.0000x reference)
import math
from contextlib import ExitStack

import numpy as np
import ml_dtypes

import concourse.bass as bass
import concourse.mybir as mybir
from concourse.bass_utils import run_bass_kernel_spmd

F32 = mybir.dt.float32
BF16 = mybir.dt.bfloat16
AF = mybir.ActivationFunctionType
ALU = mybir.AluOpType
BF = ml_dtypes.bfloat16

ENGS = ("tensor", "vector", "scalar", "sync", "gpsimd")
ENG_SEM = {"tensor": "pe", "vector": "vv", "scalar": "ss"}


class Cfg:
    def __init__(self, S=2048, D=4096, QH=16, KVH=4, ER=256, n_cores=8, eps=1e-5,
                 rope_theta=10000.0):
        self.S = S
        self.D = D
        self.QH = QH
        self.KVH = KVH
        self.GRP = QH // KVH
        self.HD = 128
        self.ER = ER
        self.NERB = ER // 128
        self.ND = D // 128
        self.NTOKB = S // 128
        self.NPASS = 2
        self.TP = S // self.NPASS
        self.TG = min(512, self.TP)
        self.NTG = self.TP // self.TG
        self.QT = 256
        self.NQT = S // self.QT
        self.MYO = D // 2
        self.OBF = min(256, self.MYO)
        self.NOBF = self.MYO // self.OBF
        self.OBW = min(512, self.MYO)
        self.NOBW = self.MYO // self.OBW
        self.NWCH = 2 * QH
        self.NTGF = 2
        self.TPF = S // self.NTGF
        self.SGW = min(512, self.TPF)
        self.n_cores = n_cores
        self.eps = eps
        self.rope_theta = rope_theta


def merge_tok(a, b):
    if a is None:
        return b
    if b is None:
        return a
    if a[0] == b[0]:
        return a if a[1] >= b[1] else b
    return b


class SB:

    def __init__(self, nc):
        self.nc = nc
        self.streams = {e: [] for e in ENGS}
        self.cnt = {}
        self.sems = {}

    def add_sem(self, stack, name):
        self.sems[name] = stack.enter_context(self.nc.semaphore(name))
        self.cnt[name] = 0

    def wait(self, eng, tok):
        if tok is None:
            return
        name, val = tok
        if val <= 0:
            return
        h = self.sems[name]
        self.streams[eng].append(lambda E, h=h, v=val: E.wait_ge(h, v))

    def waits(self, eng, toks):
        best = {}
        for t in toks:
            if t is None:
                continue
            n, v = t
            if v > best.get(n, 0):
                best[n] = v
        for n, v in best.items():
            self.wait(eng, (n, v))

    def op(self, eng, f, inc=None, n=1):
        if eng in ("vector", "scalar"):
            own = ENG_SEM[eng]
            prev = self.cnt[own]
            if prev > 0:
                self.wait(eng, (own, prev))
        tok = None
        if inc is not None:
            self.cnt[inc] += n
            tok = (inc, self.cnt[inc])
            h = self.sems[inc]
            self.streams[eng].append(lambda E, f=f, h=h, n=n: f(E).then_inc(h, n))
        else:
            self.streams[eng].append(lambda E, f=f: f(E))
        return tok

    def dma(self, eng, out, in_, sem, **kw):
        self.cnt[sem] += 16
        tok = (sem, self.cnt[sem])
        h = self.sems[sem]
        self.streams[eng].append(
            lambda E, o=out, i=in_, h=h, kw=kw: E.dma_start(out=o, in_=i, **kw).then_inc(h, 16))
        return tok

    def snap(self):
        return dict(self.cnt)

    def barrier(self, snapshot, engines=ENGS, exclude=("cc",), include=None):
        for e in engines:
            for n, v in snapshot.items():
                if include is not None and n not in include:
                    continue
                if include is None and n in exclude:
                    continue
                if v > 0:
                    self.wait(e, (n, v))


def build_nc(cfg, debug=False):
    c = cfg
    nc = bass.Bass("TRN2", target_bir_lowering=False, debug=debug)

    x_in = nc.dram_tensor("x_b", [c.S, c.D], F32, kind="ExternalInput").ap()
    x_oc = nc.dram_tensor("x_oc", [c.S, c.MYO], F32, kind="ExternalInput").ap()
    wq_t = nc.dram_tensor("wq_t", [c.QH, c.ND, 128, 128], F32, kind="ExternalInput").ap()
    wk_t = nc.dram_tensor("wk_t", [c.KVH, c.ND, 128, 128], F32, kind="ExternalInput").ap()
    wv_t = nc.dram_tensor("wv_t", [c.ND, 128, c.KVH * 128], F32, kind="ExternalInput").ap()
    wo_t = nc.dram_tensor("wo_t", [c.NOBW, c.NWCH, 128, c.OBW], F32, kind="ExternalInput").ap()
    wt_t = nc.dram_tensor("wt_t", [c.NOBF, c.ND, 128, c.OBF], F32, kind="ExternalInput").ap()
    at_t = nc.dram_tensor("at_t", [c.NERB, c.ND, 128, 128], F32, kind="ExternalInput").ap()
    bft_t = nc.dram_tensor("bft_t", [c.NOBF, c.NERB, 128, c.OBF], F32, kind="ExternalInput").ap()
    bias_in = nc.dram_tensor("bias_in", [1, c.MYO], F32, kind="ExternalInput").ap()
    cf_rep = nc.dram_tensor("cf_rep", [128, c.NERB], F32, kind="ExternalInput").ap()
    sc_rep = nc.dram_tensor("sc_rep", [128, c.NERB], F32, kind="ExternalInput").ap()
    cosq_in = nc.dram_tensor("cosq", [128, c.S], F32, kind="ExternalInput").ap()
    sinq_in = nc.dram_tensor("sinq", [128, c.S], F32, kind="ExternalInput").ap()
    cosk_in = nc.dram_tensor("cosk", [128, c.S], F32, kind="ExternalInput").ap()
    sink_in = nc.dram_tensor("sink", [128, c.S], F32, kind="ExternalInput").ap()
    GQ = c.GRP * c.QT
    m0_in = nc.dram_tensor("mask0", [128, GQ], BF16, kind="ExternalInput").ap()
    m1_in = nc.dram_tensor("mask1", [128, GQ], BF16, kind="ExternalInput").ap()
    rt_in = nc.dram_tensor("rt", [128, 128], BF16, kind="ExternalInput").ap()
    out_ext = nc.dram_tensor("out", [c.S, c.MYO], F32, kind="ExternalOutput").ap()

    xn_scr = nc.dram_tensor("xn_scr", [c.S, c.D], BF16)
    qt_scr = nc.dram_tensor("qt_scr", [c.KVH, c.NQT, 128, c.GRP * c.QT], BF16)
    cc_in = nc.dram_tensor("cc_in", [c.QH * 128, c.S], BF16)
    cc_out = nc.dram_tensor("cc_out", [c.KVH, 2 * c.GRP * 128, c.S], BF16)
    ffn_scr = nc.dram_tensor("ffn_scr", [c.S, c.MYO], F32)
    wo_bf = nc.dram_tensor("wo_bf", [c.NOBW, c.NWCH, 128, c.OBW], BF16)

    with ExitStack() as top:
        sb = SB(nc)
        dma_sems = (["dcn", "dbs", "dwc", "dtb", "dwv", "dxt", "dam", "dac"]
                    + [f"dx{i}" for i in range(4)]
                    + [f"dat{i}" for i in range(4)]
                    + [f"{p}{i}" for p in ("dxn", "dff", "dou", "dq",
                                           "dqp", "dfl", "dxr", "dwb", "dwt", "dbf",
                                           "dwo") for i in range(2)])
        for s in ("pe", "vv", "ss", "gp", "cc") + tuple(dma_sems):
            sb.add_sem(top, s)

        ktab = top.enter_context(nc.sbuf_tensor("ktab", [128, c.KVH, c.S], BF16))
        vbuf = top.enter_context(nc.sbuf_tensor("vbuf", [128, c.NTOKB, c.KVH * 128], BF16))
        m0 = top.enter_context(nc.sbuf_tensor("m0", [128, GQ], BF16))
        m1 = top.enter_context(nc.sbuf_tensor("m1", [128, GQ], BF16))
        rt = top.enter_context(nc.sbuf_tensor("rtm", [128, 128], BF16))
        ones_f = top.enter_context(nc.sbuf_tensor("ones_f", [128, 128], F32))
        ones_bf = top.enter_context(nc.sbuf_tensor("ones_bf", [1, 128], BF16))
        ones_bb = top.enter_context(nc.sbuf_tensor("ones_bb", [128, 128], BF16))
        bias_bf = top.enter_context(nc.sbuf_tensor("bias_bf", [1, c.MYO], BF16))
        wer = top.enter_context(nc.sbuf_tensor("wer", [128, c.NERB], F32))
        cfs = top.enter_context(nc.sbuf_tensor("cfs", [128, 2 * c.NERB], F32))
        csts = top.enter_context(nc.sbuf_tensor("csts", [128, 3], F32))
        xat = top.enter_context(nc.sbuf_tensor("xat", [128, c.NERB, c.S], BF16))

        P0s = ExitStack()
        tabs = P0s.enter_context(nc.sbuf_tensor("tabs", [128, 4, c.TP], F32))
        xnt = P0s.enter_context(nc.sbuf_tensor("xnt", [128, c.ND, c.TP], BF16))
        wbuf = P0s.enter_context(nc.sbuf_tensor("wbuf", [128, 2, c.ND * 128], BF16))
        qb = P0s.enter_context(nc.sbuf_tensor("qb", [128, 2, c.TG], BF16))
        tt1 = P0s.enter_context(nc.sbuf_tensor("tt1", [128, c.TG], F32))
        tt2 = P0s.enter_context(nc.sbuf_tensor("tt2", [128, c.TG], F32))
        qto = P0s.enter_context(nc.sbuf_tensor("qto", [128, 2, c.TG], BF16))

        sb.dma("sync", m0[:, :], m0_in, "dcn")
        sb.dma("sync", m1[:, :], m1_in, "dcn")
        sb.dma("sync", rt[:, :], rt_in, "dcn")
        t_cf = sb.dma("sync", cfs[:, 0:c.NERB], cf_rep, "dcn")
        t_sc = sb.dma("sync", cfs[:, c.NERB:2 * c.NERB], sc_rep, "dcn")
        t_ones = sb.op("gpsimd", lambda E: E.memset(ones_f[:, :], 1.0), inc="gp", n=1)
        t_onbb = sb.op("gpsimd", lambda E: E.memset(ones_bb[:, :], 1.0), inc="gp", n=1)
        sb.op("gpsimd", lambda E: E.memset(csts[:, 0:1], float(c.eps)), inc="gp", n=1)
        sb.op("gpsimd", lambda E: E.memset(csts[:, 1:2], -0.5), inc="gp", n=1)
        t_csts = sb.op("gpsimd", lambda E: E.memset(csts[:, 2:3], -1.0), inc="gp", n=1)
        t_ones2 = sb.op("gpsimd", lambda E: E.memset(ones_bf[:, :], 1.0), inc="gp", n=1)
        t_bias = sb.dma("gpsimd", bias_bf[:, :], bias_in, "dbs")
        tab0_tok = [sb.dma("sync", tabs[:, 0, :], cosq_in[:, 0:c.TP], "dtb"),
                    sb.dma("sync", tabs[:, 1, :], sinq_in[:, 0:c.TP], "dtb"),
                    sb.dma("sync", tabs[:, 2, :], cosk_in[:, 0:c.TP], "dtb"),
                    sb.dma("sync", tabs[:, 3, :], sink_in[:, 0:c.TP], "dtb")]
        pre_w = [sb.dma("gpsimd",
                        wbuf[:, i, :].rearrange("p (c f) -> p c f", c=c.ND),
                        wk_t[i].rearrange("c p f -> p c f"), f"dwb{i}")
                 for i in range(min(2, c.KVH))]
        sb.waits("vector", [t_cf, t_sc])
        t_wer = sb.op("vector", lambda E: E.tensor_tensor(
            wer[:, :], cfs[:, 0:c.NERB], cfs[:, c.NERB:2 * c.NERB], op=ALU.mult), inc="vv")

        NCH = c.D // 512
        xnwrite = [None] * c.NTOKB
        with ExitStack() as P1:
            xtl = P1.enter_context(nc.sbuf_tensor("xtl", [128, 2, c.D], F32))
            xnb = P1.enter_context(nc.sbuf_tensor("xnb", [128, 2, c.D], BF16))
            stats = P1.enter_context(nc.sbuf_tensor("stats", [128, NCH * 6], F32))
            mvs = P1.enter_context(nc.sbuf_tensor("mvs", [128, 2 * c.NTOKB], F32))
            lnv = P1.enter_context(nc.sbuf_tensor("lnv", [128, 1], F32))
            rstds = P1.enter_context(nc.sbuf_tensor("rstds", [128, c.NTOKB], F32))
            mrs = P1.enter_context(nc.sbuf_tensor("mrs", [128, c.NTOKB], F32))

            free_x = [None, None]
            free_xn = [None, None]
            t_aggr = [None] * c.NTOKB
            t_rstd = [None] * c.NTOKB

            def emit_apply(i):
                sl = i % 2
                sb.waits("vector", [t_rstd[i], free_xn[sl]])
                t_mr = sb.op("vector", lambda E, i=i: E.tensor_tensor(
                    mrs[:, i:i + 1], mvs[:, 2 * i:2 * i + 1], rstds[:, i:i + 1],
                    op=ALU.mult), inc="vv")
                t_ap = sb.op("vector", lambda E, i=i, sl=sl: E.tensor_scalar(
                    xnb[:, sl, :], xtl[:, sl, :], rstds[:, i:i + 1], mrs[:, i:i + 1],
                    op0=ALU.mult, op1=ALU.subtract), inc="vv")
                free_x[sl] = t_ap
                sb.waits("sync", [t_ap])
                xnwrite[i] = sb.dma("sync", xn_scr.ap()[i * 128:(i + 1) * 128, :],
                                    xnb[:, sl, :], f"dxn{sl}")
                free_xn[sl] = xnwrite[i]

            xnt_toks = [None] * c.NPASS

            def emit_xnt(p):
                nw = [xnwrite[t] for t in range((p + 1) * (c.NTOKB // c.NPASS))]
                sb.waits("sync", nw[-2:] + ([xnt_toks[p - 1]] if p else []))
                tk = []
                for ch in range(c.ND):
                    srcx = xn_scr.ap()[p * c.TP:(p + 1) * c.TP, ch * 128:(ch + 1) * 128]
                    tk.append(sb.dma("sync", xnt[:, ch, :], srcx, "dxt", transpose=True))
                xnt_toks[p] = tk[-1]

            half_tiles = c.NTOKB // c.NPASS
            for i in range(c.NTOKB):
                sl = i % 2
                sb.waits("sync", [free_x[sl]])
                t_x = sb.dma("sync", xtl[:, sl, :], x_in[i * 128:(i + 1) * 128, :],
                             f"dx{sl}")
                if i >= 1:
                    emit_apply(i - 1)
                if i == half_tiles:
                    emit_xnt(0)
                sb.waits("vector", [t_x])
                for ch in range(NCH):
                    sb.op("vector", lambda E, sl=sl, ch=ch: E.bn_stats(
                        stats[:, ch * 6:(ch + 1) * 6], xtl[:, sl, ch * 512:(ch + 1) * 512]),
                        inc="vv")
                t_aggr[i] = sb.op("vector", lambda E, i=i: E.bn_aggr(
                    mvs[:, 2 * i:2 * i + 2], stats[:, :]), inc="vv")
                sb.waits("scalar", [t_aggr[i]] + ([t_csts] if i == 0 else []))
                sb.op("scalar", lambda E, i=i: E.activation(
                    lnv[:, :], mvs[:, 2 * i + 1:2 * i + 2], AF.Ln, bias=csts[:, 0:1]),
                    inc="ss")
                t_rstd[i] = sb.op("scalar", lambda E, i=i: E.activation(
                    rstds[:, i:i + 1], lnv[:, :], AF.Exp, scale=csts[:, 1:2]), inc="ss")
            emit_apply(c.NTOKB - 1)

        p1_snap = sb.snap()
        q_dmas = []
        q_head_tok = {}
        with ExitStack() as P2:
            wvb = P2.enter_context(nc.sbuf_tensor("wvb", [128, c.ND, c.KVH * 128], BF16))
            atb = P2.enter_context(nc.sbuf_tensor("atb", [128, c.NERB, c.ND * 128], BF16))
            psQ = [P2.enter_context(nc.psum_tensor(f"psQ{i}", [128, c.TG], F32)) for i in range(2)]
            psR = [P2.enter_context(nc.psum_tensor(f"psR{i}", [128, c.TG], F32)) for i in range(2)]
            psV = [P2.enter_context(nc.psum_tensor(f"psV{i}", [128, c.TG], F32)) for i in range(2)]

            t_wv = None
            free_wslot = [None, None]
            free_qb = [None, None]
            free_psQ = [None, None]
            free_psR = [None, None]
            free_psV = [None, None]
            free_qto = [None, None]
            free_xnt = None
            free_t1 = None
            free_t2 = None
            free_tabs = None
            tile_idx = 0
            vtile_idx = 0
            widx = 0

            for p in range(c.NPASS):
                if p == 0:
                    t_tab = tab0_tok
                else:
                    sb.waits("sync", [free_tabs])
                    t_tab = [sb.dma("sync", tabs[:, 0, :], cosq_in[:, p * c.TP:(p + 1) * c.TP], "dtb"),
                             sb.dma("sync", tabs[:, 1, :], sinq_in[:, p * c.TP:(p + 1) * c.TP], "dtb"),
                             sb.dma("sync", tabs[:, 2, :], cosk_in[:, p * c.TP:(p + 1) * c.TP], "dtb"),
                             sb.dma("sync", tabs[:, 3, :], sink_in[:, p * c.TP:(p + 1) * c.TP], "dtb")]
                    sb.waits("sync", [free_xnt])
                    emit_xnt(p)
                t_xnt = xnt_toks[p]

                colbs = [("k", i) for i in range(c.KVH)] + [("q", i) for i in range(c.QH)]
                for (kind, hidx) in colbs:
                    wsl = widx % 2
                    csel = wq_t[hidx] if kind == "q" else wk_t[hidx]
                    if widx < len(pre_w):
                        t_w = pre_w[widx]
                    else:
                        sb.waits("gpsimd", [free_wslot[wsl]])
                        t_w = sb.dma("gpsimd",
                                     wbuf[:, wsl, :].rearrange("p (c f) -> p c f", c=c.ND),
                                     csel.rearrange("c p f -> p c f"), f"dwb{wsl}")
                    last_mm_colb = None
                    for tg in range(c.NTG):
                        sl = tile_idx % 2
                        tloc = tg * c.TG
                        sb.waits("tensor", [t_w, t_xnt, free_psQ[sl]])
                        t_proj = None
                        for ch in range(c.ND):
                            t_proj = sb.op("tensor", lambda E, sl=sl, wsl=wsl, ch=ch, tg=tg: E.matmul(
                                psQ[sl][:, :], wbuf[:, wsl, ch * 128:(ch + 1) * 128],
                                xnt[:, ch, tg * c.TG:(tg + 1) * c.TG],
                                start=(ch == 0), stop=(ch == c.ND - 1)),
                                inc="pe" if ch == c.ND - 1 else None)
                        last_mm_colb = t_proj
                        sb.waits("scalar", [t_proj, free_qb[sl]])
                        t_qb = sb.op("scalar", lambda E, sl=sl: E.activation(
                            qb[:, sl, :], psQ[sl][:, :], AF.Copy), inc="ss")
                        free_psQ[sl] = t_qb
                        sb.waits("tensor", [t_qb, free_psR[sl]])
                        t_rot = sb.op("tensor", lambda E, sl=sl: E.matmul(
                            psR[sl][:, :], rt[:, :], qb[:, sl, :], start=True, stop=True),
                            inc="pe")
                        ti = 0 if kind == "q" else 2
                        sb.waits("vector", [t_qb, free_t1] + (t_tab if tile_idx == 0 or True else []))
                        tv1 = sb.op("vector", lambda E, sl=sl, ti=ti, tloc=tloc: E.tensor_tensor(
                            tt1[:, :], qb[:, sl, :], tabs[:, ti, tloc:tloc + c.TG], op=ALU.mult),
                            inc="vv")
                        free_qb[sl] = tv1
                        sb.waits("vector", [t_rot, free_t2])
                        tv2 = sb.op("vector", lambda E, sl=sl, ti=ti, tloc=tloc: E.tensor_tensor(
                            tt2[:, :], psR[sl][:, :], tabs[:, ti + 1, tloc:tloc + c.TG], op=ALU.mult),
                            inc="vv")
                        free_psR[sl] = tv2
                        toks0 = p * c.TP + tloc
                        if kind == "k":
                            t_cmb = sb.op("vector", lambda E, hidx=hidx, toks0=toks0: E.tensor_tensor(
                                ktab[:, hidx, toks0:toks0 + c.TG], tt1[:, :], tt2[:, :],
                                op=ALU.add), inc="vv")
                        else:
                            sb.waits("vector", [free_qto[sl]])
                            t_cmb = sb.op("vector", lambda E, sl=sl: E.tensor_tensor(
                                qto[:, sl, :], tt1[:, :], tt2[:, :], op=ALU.add), inc="vv")
                            sb.waits("scalar", [t_cmb])
                            gq, rq = divmod(hidx, c.GRP)
                            t_dq = None
                            for j2 in range(toks0 // c.QT, (toks0 + c.TG) // c.QT):
                                off = j2 * c.QT - toks0
                                t_dq = sb.dma(
                                    "scalar",
                                    qt_scr.ap()[gq, j2, :, rq * c.QT:(rq + 1) * c.QT],
                                    qto[:, sl, off:off + c.QT], f"dq{sl}")
                                q_dmas.append(t_dq)
                                q_head_tok.setdefault(hidx, []).append(t_dq)
                            free_qto[sl] = t_dq
                        free_t1 = t_cmb
                        free_t2 = t_cmb
                        free_tabs = t_cmb
                        tile_idx += 1
                    free_wslot[wsl] = last_mm_colb
                    widx += 1

                if p == 0:
                    sb.barrier(p1_snap, engines=("gpsimd",), include=("vv", "dxn0", "dxn1"))
                    t_wv = sb.dma("gpsimd", wvb[:, :, :],
                                  wv_t.rearrange("c p f -> p c f"), "dwv")
                    t_at = sb.dma("gpsimd",
                                  atb[:, :, :].rearrange("p b (c f) -> p b c f", c=c.ND),
                                  at_t.rearrange("b c p f -> p b c f"), "dac")
                ntb = c.TP // 128
                t_pv = None
                for tb in range(ntb):
                    sl = vtile_idx % 2
                    gtokb = p * ntb + tb
                    sb.waits("tensor", [t_wv, t_xnt, free_psV[sl]])
                    for ch in range(c.ND):
                        t_pv = sb.op("tensor", lambda E, sl=sl, ch=ch, tb=tb: E.matmul(
                            psV[sl][:, 0:c.KVH * 128], xnt[:, ch, tb * 128:(tb + 1) * 128],
                            wvb[:, ch, :], start=(ch == 0), stop=(ch == c.ND - 1)),
                            inc="pe" if ch == c.ND - 1 else None)
                    sb.waits("vector", [t_pv])
                    t_ev = sb.op("vector", lambda E, sl=sl, gtokb=gtokb: E.tensor_copy(
                        vbuf[:, gtokb, :], psV[sl][:, 0:c.KVH * 128]), inc="vv")
                    free_psV[sl] = t_ev
                    vtile_idx += 1
                NSG = c.TP // c.SGW
                t_px = None
                for erb in range(c.NERB):
                    for sg in range(NSG):
                        sb.waits("tensor", [t_at, free_psV[0]] if t_px is None
                                 else [free_psV[0]])
                        for ch in range(c.ND):
                            t_px = sb.op("tensor", lambda E, erb=erb, sg=sg, ch=ch: E.matmul(
                                psV[0][:, 0:c.SGW], atb[:, erb, ch * 128:(ch + 1) * 128],
                                xnt[:, ch, sg * c.SGW:(sg + 1) * c.SGW],
                                start=(ch == 0), stop=(ch == c.ND - 1)),
                                inc="pe" if ch == c.ND - 1 else None)
                        sb.waits("vector", [t_px, t_wer])
                        col0 = p * c.TP + sg * c.SGW
                        t_s2 = sb.op("vector", lambda E, erb=erb, col0=col0: E.tensor_scalar(
                            xat[:, erb, col0:col0 + c.SGW], psV[0][:, 0:c.SGW],
                            wer[:, erb:erb + 1], None, op0=ALU.mult), inc="vv")
                        free_psV[0] = t_s2
                        t_xa_last = t_s2
                free_xnt = t_px
        P0s.close()

        p2_snap = sb.snap()
        NH = GQ // 512
        P34 = ExitStack()
        xnm = P34.enter_context(nc.sbuf_tensor("xnm", [128, c.ND, c.TPF], BF16))
        with ExitStack() as P3:
            qpack = P3.enter_context(nc.sbuf_tensor("qpack", [128, 2, GQ], BF16))
            expb = P3.enter_context(nc.sbuf_tensor("expb", [128, 4, GQ], BF16))
            expm = P3.enter_context(nc.sbuf_tensor("expm", [128, 4, GQ], BF16))
            acc4 = P3.enter_context(nc.sbuf_tensor("acc4", [128, GQ], F32))
            acc4b = P3.enter_context(nc.sbuf_tensor("acc4b", [128, GQ], BF16))
            lnd = P3.enter_context(nc.sbuf_tensor("lnd", [128, GQ], F32))
            invd = P3.enter_context(nc.sbuf_tensor("invd", [128, GQ], F32))
            atile = P3.enter_context(nc.sbuf_tensor("atile", [128, 4, GQ], BF16))
            wtb = P3.enter_context(nc.sbuf_tensor("wtb", [128, 2, c.ND * c.OBF], BF16))
            bftb = P3.enter_context(nc.sbuf_tensor("bftb", [128, 2, c.NERB * c.OBF], BF16))
            ftl = P3.enter_context(nc.sbuf_tensor("ftl", [128, 2, c.OBF], F32))
            psL = [P3.enter_context(nc.psum_tensor(f"psL{i}", [128, GQ], F32)) for i in range(2)]
            psA2 = P3.enter_context(nc.psum_tensor("psA2", [128, GQ], F32))
            psF = [P3.enter_context(nc.psum_tensor(f"psF{i}", [128, c.OBF], F32)) for i in range(2)]

            sb.barrier(p2_snap, exclude=("cc", "dwc"))

            xnm_pre = []
            for ch in range(c.ND):
                srcp = xn_scr.ap()[0:c.TPF, ch * 128:(ch + 1) * 128]
                xnm_pre.append(sb.dma("sync", xnm[:, ch, :], srcp, "dxt", transpose=True))
            t_wocast = sb.dma("gpsimd", wo_bf.ap(), wo_t, "dwc")

            free_qp = [None, None]
            free_psL = [None, None]
            free_expb = [[], [], [], []]
            free_expm = [[], [], [], []]
            free_psA2 = None
            free_acc = None
            free_accb = None
            free_atile = [None, None, None, None]
            free_invd = None
            ag_all = [None, None, None, None]
            qp_toks = {}

            def emit_qpack(it2):
                g2, j2 = divmod(it2, c.NQT)
                slq = it2 % 2
                srcq = qt_scr.ap()[g2, j2]
                need = []
                for hq in range(g2 * c.GRP, (g2 + 1) * c.GRP):
                    need += q_head_tok.get(hq, [])
                sb.waits("scalar", [free_qp[slq]] + need)
                qp_toks[it2] = sb.dma("scalar", qpack[:, slq, :], srcq, f"dqp{slq}")

            def ffn_emitter():
                CH_YIELD = 16
                nmm = 0

                def tick():
                    nonlocal nmm
                    nmm += 1
                    return nmm % CH_YIELD == 0

                free_xnm = None
                free_wt = [None, None]
                free_bft = [None, None]
                free_psF = [None, None]
                free_ftl = [None, None]
                fidx = 0
                t_f = None
                t_xnm = None
                for tgf in range(c.NTGF):
                    if tgf == 0:
                        t_xnm = xnm_pre[-1]
                    else:
                        sb.waits("sync", [free_xnm])
                        xnm_tok = []
                        for ch in range(c.ND):
                            src = xn_scr.ap()[tgf * c.TPF:(tgf + 1) * c.TPF,
                                              ch * 128:(ch + 1) * 128]
                            xnm_tok.append(sb.dma("sync", xnm[:, ch, :], src, "dxt",
                                                  transpose=True))
                        t_xnm = xnm_tok[-1]
                        for _ in range(16):
                            yield
                    for ob in range(c.NOBF):
                        wsl2 = ob % 2
                        sb.waits("gpsimd", [free_wt[wsl2]])
                        t_wt = sb.dma("gpsimd",
                                      wtb[:, wsl2, :].rearrange("p (c f) -> p c f", c=c.ND),
                                      wt_t[ob].rearrange("c p f -> p c f"), f"dwt{wsl2}")
                        sb.waits("gpsimd", [free_bft[wsl2]])
                        t_bft = sb.dma("gpsimd",
                                       bftb[:, wsl2, :].rearrange("p (b f) -> p b f", b=c.NERB),
                                       bft_t[ob].rearrange("b p f -> p b f"), f"dbf{wsl2}")
                        for tb in range(c.TPF // 128):
                            sl = fidx % 2
                            gtb = tgf * (c.TPF // 128) + tb
                            sb.waits("tensor", [t_wt, t_bft, t_xnm, free_psF[sl],
                                                t_ones2, t_bias])
                            for ch in range(c.ND):
                                sb.op("tensor", lambda E, sl=sl, ch=ch, tb=tb, wsl2=wsl2: E.matmul(
                                    psF[sl][:, :], xnm[:, ch, tb * 128:(tb + 1) * 128],
                                    wtb[:, wsl2, ch * c.OBF:(ch + 1) * c.OBF],
                                    start=(ch == 0), stop=False), inc=None)
                                if tick():
                                    yield
                            for erb in range(c.NERB):
                                sb.op("tensor", lambda E, sl=sl, erb=erb, gtb=gtb, wsl2=wsl2: E.matmul(
                                    psF[sl][:, :], xat[:, erb, gtb * 128:(gtb + 1) * 128],
                                    bftb[:, wsl2, erb * c.OBF:(erb + 1) * c.OBF],
                                    start=False, stop=False), inc=None)
                            t_f = sb.op("tensor", lambda E, sl=sl, ob=ob: E.matmul(
                                psF[sl][:, :], ones_bf[:, :],
                                bias_bf[:, ob * c.OBF:(ob + 1) * c.OBF],
                                start=False, stop=True), inc="pe")
                            sb.waits("vector", [t_f, free_ftl[sl]])
                            t_ev = sb.op("vector", lambda E, sl=sl: E.tensor_copy(
                                ftl[:, sl, :], psF[sl][:, :]), inc="vv")
                            free_psF[sl] = t_ev
                            sb.waits("sync", [t_ev])
                            dstf = ffn_scr.ap()[gtb * 128:(gtb + 1) * 128,
                                                ob * c.OBF:(ob + 1) * c.OBF]
                            t_fw = sb.dma("sync", dstf, ftl[:, sl, :], f"dff{sl}")
                            free_ftl[sl] = t_fw
                            ffn_w_all[sl] = t_fw
                            fidx += 1
                        free_wt[wsl2] = t_f
                        free_bft[wsl2] = t_f
                    free_xnm = t_f

            ffn_w_all = [None, None]
            ffn_gen = ffn_emitter()

            emit_qpack(0)
            it = 0
            for g in range(c.KVH):
                for j in range(c.NQT):
                    sl2 = it % 2
                    sl4a = it % 4
                    span = (j + 1) * (c.QT // 128)
                    if it + 1 < c.KVH * c.NQT:
                        emit_qpack(it + 1)
                    t_qp = qp_toks[it]

                    tok_acc = None
                    t_log = None
                    t_av = None
                    pend = []

                    def emit_av(w, g=g, span=span):
                        nonlocal t_av
                        kb, ksl4, esrc, t_src = w
                        sb.waits("tensor", [t_src] + ([free_psA2] if kb == 0 else []))
                        for hh in range(NH):
                            t_av = sb.op("tensor", lambda E, g=g, kb=kb, ksl4=ksl4, hh=hh,
                                         esrc=esrc, st=(kb == 0), sp=(kb == span - 1): E.matmul(
                                psA2[:, hh * 512:(hh + 1) * 512],
                                vbuf[:, kb, g * 128:(g + 1) * 128],
                                esrc[:, ksl4, hh * 512:(hh + 1) * 512],
                                start=st, stop=sp), inc="pe" if hh == NH - 1 else None)
                        if esrc is expm:
                            free_expm[ksl4] = free_expm[ksl4] + [t_av, tok_acc]
                        else:
                            free_expb[ksl4] = free_expb[ksl4] + [t_av, tok_acc]

                    for kb in range(span):
                        ksl = kb % 2
                        ksl4 = kb % 4
                        diag = (kb >= span - 2)
                        sb.waits("tensor", [t_qp, free_psL[ksl]])
                        for hh in range(NH):
                            t_log = sb.op("tensor", lambda E, g=g, kb=kb, ksl=ksl, hh=hh, sl2=sl2: E.matmul(
                                psL[ksl][:, hh * 512:(hh + 1) * 512],
                                ktab[:, g, kb * 128:(kb + 1) * 128],
                                qpack[:, sl2, hh * 512:(hh + 1) * 512],
                                start=True, stop=True), inc="pe" if hh == NH - 1 else None)
                        if len(pend) >= 3:
                            emit_av(pend.pop(0))
                        sb.waits("scalar", [t_log] + free_expb[ksl4])
                        t_exp = sb.op("scalar", lambda E, ksl4=ksl4, ksl=ksl: E.activation(
                            expb[:, ksl4, :], psL[ksl][:, :], AF.Exp), inc="ss")
                        free_expb[ksl4] = []
                        free_psL[ksl] = t_exp
                        esrc, t_src = expb, t_exp
                        if diag:
                            msk = m0 if kb == span - 2 else m1
                            sb.waits("vector", [t_exp] + free_expm[ksl4])
                            t_msk = sb.op("vector", lambda E, ksl4=ksl4, msk=msk: E.tensor_tensor(
                                expm[:, ksl4, :], expb[:, ksl4, :], msk[:, :], op=ALU.mult),
                                inc="vv")
                            free_expm[ksl4] = []
                            free_expb[ksl4] = [t_msk]
                            esrc, t_src = expm, t_msk
                        sb.waits("vector", [t_src] + ([free_acc] if kb == 0 else []))
                        if kb == 0:
                            tok_acc = sb.op("vector", lambda E, esrc=esrc, ksl4=ksl4: E.tensor_copy(
                                acc4[:, :], esrc[:, ksl4, :]), inc="vv")
                        else:
                            tok_acc = sb.op("vector", lambda E, esrc=esrc, ksl4=ksl4: E.tensor_tensor(
                                acc4[:, :], acc4[:, :], esrc[:, ksl4, :], op=ALU.add), inc="vv")
                        pend.append((kb, ksl4, esrc, t_src))
                        next(ffn_gen, None)
                    while pend:
                        emit_av(pend.pop(0))
                    tok_av_last = t_av
                    sb.waits("vector", [tok_acc, free_accb])
                    t_accb = sb.op("vector", lambda E: E.tensor_copy(
                        acc4b[:, :], acc4[:, :]), inc="vv")
                    free_acc = t_accb
                    dsl = span % 2
                    sb.waits("tensor", [t_accb, t_onbb, free_psL[dsl]])
                    t_dmm = None
                    for hh in range(NH):
                        t_dmm = sb.op("tensor", lambda E, hh=hh, dsl=dsl: E.matmul(
                            psL[dsl][:, hh * 512:(hh + 1) * 512], ones_bb[:, :],
                            acc4b[:, hh * 512:(hh + 1) * 512], start=True, stop=True),
                            inc="pe" if hh == NH - 1 else None)
                    free_accb = t_dmm
                    sb.waits("scalar", [t_dmm])
                    t_ln2 = sb.op("scalar", lambda E, dsl=dsl: E.activation(
                        lnd[:, :], psL[dsl][:, :], AF.Ln), inc="ss")
                    free_psL[dsl] = t_ln2
                    sb.waits("scalar", [free_invd])
                    t_inv = sb.op("scalar", lambda E: E.activation(
                        invd[:, :], lnd[:, :], AF.Exp, scale=csts[:, 2:3]), inc="ss")
                    sb.waits("vector", [t_inv, tok_av_last, free_atile[sl4a]])
                    t_nrm = sb.op("vector", lambda E, sl4a=sl4a: E.tensor_tensor(
                        atile[:, sl4a, :], psA2[:, :], invd[:, :], op=ALU.mult), inc="vv")
                    free_psA2 = t_nrm
                    free_invd = t_nrm
                    dst = cc_in.ap().rearrange("(h d) q -> d h q", d=128)[
                        :, g * c.GRP:(g + 1) * c.GRP, j * c.QT:(j + 1) * c.QT]
                    sb.waits("sync", [t_nrm])
                    t_wr = sb.dma("sync", dst, atile[:, sl4a, :], f"dat{sl4a}")
                    free_atile[sl4a] = t_wr
                    ag_all[sl4a] = t_wr
                    free_qp[sl2] = t_log
                    it += 1
                sb.waits("gpsimd", ag_all)
                rg = [[2 * i, 2 * i + 1] for i in range(c.n_cores // 2)]
                gr = c.GRP * 128
                sb.op("gpsimd", lambda E, g=g, gr=gr: E.collective_compute(
                    "AllGather", ALU.bypass, replica_groups=rg,
                    ins=[cc_in.ap()[g * gr:(g + 1) * gr, :]],
                    outs=[cc_out.ap()[g]]), inc="cc", n=1)

            for _ in ffn_gen:
                pass

        P34.close()

        p4_snap = sb.snap()
        with ExitStack() as P5:
            atm = P5.enter_context(nc.sbuf_tensor("atm", [128, c.NWCH, c.TPF], BF16))
            wob = P5.enter_context(nc.sbuf_tensor("wob", [128, 2, c.NWCH * c.OBW], BF16))
            ftl2 = P5.enter_context(nc.sbuf_tensor("ftl2", [128, 2, c.OBW], F32))
            xrl = P5.enter_context(nc.sbuf_tensor("xrl", [128, 2, c.OBW], F32))
            otl = P5.enter_context(nc.sbuf_tensor("otl", [128, 2, c.OBW], F32))
            psW = [P5.enter_context(nc.psum_tensor(f"psW{i}", [128, c.OBW], F32)) for i in range(2)]

            sb.barrier(p4_snap, exclude=("cc",))
            cc_tok = ("cc", sb.cnt["cc"])

            def cco_src(ch, tgf):
                half, hl = divmod(ch, c.QH)
                g2, hr = divmod(hl, c.GRP)
                row = (half * c.GRP + hr) * 128
                return cc_out.ap()[g2, row:row + 128, tgf * c.TPF:(tgf + 1) * c.TPF]

            free_atm = None
            free_wo = [None, None]
            free_psW = [None, None]
            free_ftl2 = [None, None]
            free_xrl = [None, None]
            free_otl = [None, None]
            widx2 = 0
            t_ow = None
            t_wmm = None
            for tgf in range(c.NTGF):
                sb.waits("sync", [cc_tok, free_atm])
                atm_tok = []
                for ch in range(c.NWCH):
                    atm_tok.append(sb.dma("sync", atm[:, ch, :], cco_src(ch, tgf), "dam"))
                t_atm = atm_tok[-1]
                for ob in range(c.NOBW):
                    wsl = (tgf * c.NOBW + ob) % 2
                    sb.waits("gpsimd", [t_wocast, free_wo[wsl]])
                    t_wo = sb.dma("gpsimd",
                                  wob[:, wsl, :].rearrange("p (c f) -> p c f", c=c.NWCH),
                                  wo_bf.ap()[ob].rearrange("c p f -> p c f"), f"dwo{wsl}")
                    for tb in range(c.TPF // 128):
                        sl = widx2 % 2
                        gtb = tgf * (c.TPF // 128) + tb
                        sb.waits("scalar", ffn_w_all if widx2 == 0 else [])
                        sb.waits("scalar", [free_ftl2[sl]])
                        srcf = ffn_scr.ap()[gtb * 128:(gtb + 1) * 128,
                                            ob * c.OBW:(ob + 1) * c.OBW]
                        t_lf = sb.dma("scalar", ftl2[:, sl, :], srcf, f"dfl{sl}")
                        sb.waits("scalar", [free_xrl[sl]])
                        srcx = x_oc[gtb * 128:(gtb + 1) * 128, ob * c.OBW:(ob + 1) * c.OBW]
                        t_lx = sb.dma("scalar", xrl[:, sl, :], srcx, f"dxr{sl}")
                        sb.waits("tensor", [t_wo, t_atm, free_psW[sl]])
                        for ch in range(c.NWCH):
                            t_wmm = sb.op("tensor", lambda E, wsl=wsl, sl=sl, ch=ch, tb=tb: E.matmul(
                                psW[sl][:, :], atm[:, ch, tb * 128:(tb + 1) * 128],
                                wob[:, wsl, ch * c.OBW:(ch + 1) * c.OBW],
                                start=(ch == 0), stop=(ch == c.NWCH - 1)),
                                inc="pe" if ch == c.NWCH - 1 else None)
                        sb.waits("vector", [t_wmm, t_lf, free_otl[sl]])
                        t_a1 = sb.op("vector", lambda E, sl=sl: E.tensor_tensor(
                            otl[:, sl, :], psW[sl][:, :], ftl2[:, sl, :], op=ALU.add), inc="vv")
                        free_psW[sl] = t_a1
                        free_ftl2[sl] = t_a1
                        sb.waits("vector", [t_lx])
                        t_a2 = sb.op("vector", lambda E, sl=sl: E.tensor_tensor(
                            otl[:, sl, :], otl[:, sl, :], xrl[:, sl, :], op=ALU.add), inc="vv")
                        free_xrl[sl] = t_a2
                        sb.waits("sync", [t_a2])
                        dsto = out_ext[gtb * 128:(gtb + 1) * 128,
                                       ob * c.OBW:(ob + 1) * c.OBW]
                        t_ow = sb.dma("sync", dsto, otl[:, sl, :], f"dou{sl}")
                        free_otl[sl] = t_ow
                        widx2 += 1
                    free_wo[wsl] = t_wmm
                free_atm = t_wmm

            sb.waits("sync", [free_otl[0], free_otl[1]])

        with nc.Block() as block:
            @block.tensor
            def _(E):
                for f in sb.streams["tensor"]:
                    f(E)

            @block.vector
            def _(E):
                for f in sb.streams["vector"]:
                    f(E)

            @block.scalar
            def _(E):
                for f in sb.streams["scalar"]:
                    f(E)

            @block.sync
            def _(E):
                for f in sb.streams["sync"]:
                    f(E)

            @block.gpsimd
            def _(E):
                for f in sb.streams["gpsimd"]:
                    f(E)

    return nc



def _rope_tables(S, HD, theta):
    inv = 1.0 / theta ** (np.arange(0, HD, 2, dtype=np.float64) / HD)
    t = np.arange(S, dtype=np.float64)
    freqs = np.outer(t, inv)
    emb = np.concatenate([freqs, freqs], axis=-1)
    return np.cos(emb).T.astype(np.float32), np.sin(emb).T.astype(np.float32)


def make_core_inputs(c, x_b, Wq_h, Wk_h, Wv_h, Wo_h, WT_h, AT, BfT_h, bias_h,
                     coeffs, scales, x_oc):
    cos, sin = _rope_tables(c.S, c.HD, c.rope_theta)
    s = 1.0 / math.sqrt(c.HD)
    m0 = (np.arange(128)[:, None] <= np.arange(c.QT)[None, :]).astype(np.float32)
    m1 = ((np.arange(128)[:, None] + 128) <= np.arange(c.QT)[None, :]).astype(np.float32)
    m0 = np.ascontiguousarray(np.tile(m0, (1, c.GRP))).astype(BF)
    m1 = np.ascontiguousarray(np.tile(m1, (1, c.GRP))).astype(BF)
    R = np.zeros((128, 128), np.float32)
    hh = 64
    for d in range(hh):
        R[d, d + hh] = -1.0
    for d in range(hh, 128):
        R[d, d - hh] = 1.0
    rt = np.ascontiguousarray(R.T).astype(BF)

    return {
        "x_b": np.ascontiguousarray(x_b),
        "x_oc": np.ascontiguousarray(x_oc),
        "wq_t": np.ascontiguousarray(
            Wq_h.reshape(c.ND, 128, c.QH, 128).transpose(2, 0, 1, 3)),
        "wk_t": np.ascontiguousarray(
            Wk_h.reshape(c.ND, 128, c.KVH, 128).transpose(2, 0, 1, 3)),
        "wv_t": np.ascontiguousarray(Wv_h.reshape(c.ND, 128, c.KVH * 128)),
        "wo_t": np.ascontiguousarray(
            Wo_h.reshape(c.NWCH, 128, c.NOBW, c.OBW).transpose(2, 0, 1, 3)),
        "wt_t": np.ascontiguousarray(
            WT_h.reshape(c.ND, 128, c.NOBF, c.OBF).transpose(2, 0, 1, 3)),
        "at_t": np.ascontiguousarray(
            AT.reshape(c.ND, 128, c.NERB, 128).transpose(2, 0, 1, 3)),
        "bft_t": np.ascontiguousarray(
            BfT_h.reshape(c.NERB, 128, c.NOBF, c.OBF).transpose(2, 0, 1, 3)),
        "bias_in": np.ascontiguousarray(bias_h.reshape(1, c.MYO)),
        "cf_rep": np.ascontiguousarray(
            np.repeat(coeffs, c.ER // len(coeffs)).reshape(c.NERB, 128).T),
        "sc_rep": np.ascontiguousarray(
            np.repeat(scales, c.ER // len(scales)).reshape(c.NERB, 128).T),
        "cosq": np.ascontiguousarray(cos * s),
        "sinq": np.ascontiguousarray(sin * s),
        "cosk": np.ascontiguousarray(cos),
        "sink": np.ascontiguousarray(sin),
        "mask0": m0,
        "mask1": m1,
        "rt": rt,
    }


_NC_CACHE = {}


def kernel(L0_attn_hidden_states, L0_attn_Wq, L0_attn_Wk, L0_attn_Wv, L0_attn_Wo,
           L0_ffn_W, L0_ffn_bias, L0_ffn_A, L0_ffn_Bf, L0_ffn_coeffs, L0_ffn_scales):
    x = np.asarray(L0_attn_hidden_states, dtype=np.float32)
    Wq = np.asarray(L0_attn_Wq, np.float32)
    Wk = np.asarray(L0_attn_Wk, np.float32)
    Wv = np.asarray(L0_attn_Wv, np.float32)
    Wo = np.asarray(L0_attn_Wo, np.float32)
    W = np.asarray(L0_ffn_W, np.float32)
    bias = np.asarray(L0_ffn_bias, np.float32)
    A = np.asarray(L0_ffn_A, np.float32)
    Bf = np.asarray(L0_ffn_Bf, np.float32)
    coeffs = np.asarray(L0_ffn_coeffs, np.float32)
    scales = np.asarray(L0_ffn_scales, np.float32)

    B, S, D = x.shape
    c = Cfg(S=S, D=D)
    WT = np.ascontiguousarray(W.T)
    AT = np.ascontiguousarray(A.reshape(c.ER, D).T)
    BfT = np.ascontiguousarray(Bf.transpose(0, 2, 1).reshape(c.ER, D))

    key = (S, D)
    if key not in _NC_CACHE:
        _NC_CACHE[key] = build_nc(c)
    nc = _NC_CACHE[key]

    in_maps = []
    for core in range(8):
        b, h = core // 2, core % 2
        qs = c.QH * 128
        ks = c.KVH * 128
        o0 = h * c.MYO
        im = make_core_inputs(
            c, x[b],
            Wq[:, h * qs:(h + 1) * qs],
            Wk[:, h * ks:(h + 1) * ks],
            Wv[:, h * ks:(h + 1) * ks],
            Wo[:, o0:o0 + c.MYO],
            WT[:, o0:o0 + c.MYO],
            AT,
            BfT[:, o0:o0 + c.MYO],
            bias[o0:o0 + c.MYO],
            coeffs, scales,
            x[b][:, o0:o0 + c.MYO],
        )
        in_maps.append(im)

    res = run_bass_kernel_spmd(nc, in_maps, core_ids=list(range(8)))
    out = np.empty((B, S, D), np.float32)
    for core in range(8):
        b, h = core // 2, core % 2
        out[b, :, h * c.MYO:(h + 1) * c.MYO] = res.results[core]["out"]
    return out


# version 28
# speedup vs baseline: 1.0218x; 1.0218x over previous
import math
from contextlib import ExitStack

import numpy as np
import ml_dtypes

import concourse.bass as bass
import concourse.mybir as mybir
from concourse.bass_utils import run_bass_kernel_spmd

F32 = mybir.dt.float32
BF16 = mybir.dt.bfloat16
AF = mybir.ActivationFunctionType
ALU = mybir.AluOpType
BF = ml_dtypes.bfloat16

ENGS = ("tensor", "vector", "scalar", "sync", "gpsimd")
ENG_SEM = {"tensor": "pe", "vector": "vv", "scalar": "ss"}


class Cfg:
    def __init__(self, S=2048, D=4096, QH=16, KVH=4, ER=256, n_cores=8, eps=1e-5,
                 rope_theta=10000.0):
        self.S = S
        self.D = D
        self.QH = QH
        self.KVH = KVH
        self.GRP = QH // KVH
        self.HD = 128
        self.ER = ER
        self.NERB = ER // 128
        self.ND = D // 128
        self.NTOKB = S // 128
        self.NPASS = 2
        self.TP = S // self.NPASS
        self.TG = min(512, self.TP)
        self.NTG = self.TP // self.TG
        self.QT = 256
        self.NQT = S // self.QT
        self.MYO = D // 2
        self.OBF = min(256, self.MYO)
        self.NOBF = self.MYO // self.OBF
        self.OBW = min(512, self.MYO)
        self.NOBW = self.MYO // self.OBW
        self.NWCH = 2 * QH
        self.NTGF = 2
        self.TPF = S // self.NTGF
        self.SGW = min(512, self.TPF)
        self.n_cores = n_cores
        self.eps = eps
        self.rope_theta = rope_theta


def merge_tok(a, b):
    if a is None:
        return b
    if b is None:
        return a
    if a[0] == b[0]:
        return a if a[1] >= b[1] else b
    return b


class SB:

    def __init__(self, nc):
        self.nc = nc
        self.streams = {e: [] for e in ENGS}
        self.cnt = {}
        self.sems = {}

    def add_sem(self, stack, name):
        self.sems[name] = stack.enter_context(self.nc.semaphore(name))
        self.cnt[name] = 0

    def wait(self, eng, tok):
        if tok is None:
            return
        name, val = tok
        if val <= 0:
            return
        h = self.sems[name]
        self.streams[eng].append(lambda E, h=h, v=val: E.wait_ge(h, v))

    def waits(self, eng, toks):
        best = {}
        for t in toks:
            if t is None:
                continue
            n, v = t
            if v > best.get(n, 0):
                best[n] = v
        for n, v in best.items():
            self.wait(eng, (n, v))

    def op(self, eng, f, inc=None, n=1):
        if eng in ("vector", "scalar"):
            own = ENG_SEM[eng]
            prev = self.cnt[own]
            if prev > 0:
                self.wait(eng, (own, prev))
        tok = None
        if inc is not None:
            self.cnt[inc] += n
            tok = (inc, self.cnt[inc])
            h = self.sems[inc]
            self.streams[eng].append(lambda E, f=f, h=h, n=n: f(E).then_inc(h, n))
        else:
            self.streams[eng].append(lambda E, f=f: f(E))
        return tok

    def dma(self, eng, out, in_, sem, **kw):
        self.cnt[sem] += 16
        tok = (sem, self.cnt[sem])
        h = self.sems[sem]
        self.streams[eng].append(
            lambda E, o=out, i=in_, h=h, kw=kw: E.dma_start(out=o, in_=i, **kw).then_inc(h, 16))
        return tok

    def snap(self):
        return dict(self.cnt)

    def barrier(self, snapshot, engines=ENGS, exclude=("cc",), include=None):
        for e in engines:
            for n, v in snapshot.items():
                if include is not None and n not in include:
                    continue
                if include is None and n in exclude:
                    continue
                if v > 0:
                    self.wait(e, (n, v))


def build_nc(cfg, debug=False):
    c = cfg
    nc = bass.Bass("TRN2", target_bir_lowering=False, debug=debug)

    x_in = nc.dram_tensor("x_b", [c.S, c.D], F32, kind="ExternalInput").ap()
    x_oc = nc.dram_tensor("x_oc", [c.S, c.MYO], F32, kind="ExternalInput").ap()
    wq_t = nc.dram_tensor("wq_t", [c.QH, c.ND, 128, 128], F32, kind="ExternalInput").ap()
    wk_t = nc.dram_tensor("wk_t", [c.KVH, c.ND, 128, 128], F32, kind="ExternalInput").ap()
    wv_t = nc.dram_tensor("wv_t", [c.ND, 128, c.KVH * 128], F32, kind="ExternalInput").ap()
    wo_t = nc.dram_tensor("wo_t", [c.NOBW, c.NWCH, 128, c.OBW], F32, kind="ExternalInput").ap()
    wt_t = nc.dram_tensor("wt_t", [c.NOBF, c.ND, 128, c.OBF], F32, kind="ExternalInput").ap()
    at_t = nc.dram_tensor("at_t", [c.NERB, c.ND, 128, 128], F32, kind="ExternalInput").ap()
    bft_t = nc.dram_tensor("bft_t", [c.NOBF, c.NERB, 128, c.OBF], F32, kind="ExternalInput").ap()
    bias_in = nc.dram_tensor("bias_in", [1, c.MYO], F32, kind="ExternalInput").ap()
    cf_rep = nc.dram_tensor("cf_rep", [128, c.NERB], F32, kind="ExternalInput").ap()
    sc_rep = nc.dram_tensor("sc_rep", [128, c.NERB], F32, kind="ExternalInput").ap()
    cosq_in = nc.dram_tensor("cosq", [128, c.S], F32, kind="ExternalInput").ap()
    sinq_in = nc.dram_tensor("sinq", [128, c.S], F32, kind="ExternalInput").ap()
    cosk_in = nc.dram_tensor("cosk", [128, c.S], F32, kind="ExternalInput").ap()
    sink_in = nc.dram_tensor("sink", [128, c.S], F32, kind="ExternalInput").ap()
    GQ = c.GRP * c.QT
    m0_in = nc.dram_tensor("mask0", [128, GQ], BF16, kind="ExternalInput").ap()
    m1_in = nc.dram_tensor("mask1", [128, GQ], BF16, kind="ExternalInput").ap()
    rt_in = nc.dram_tensor("rt", [128, 128], BF16, kind="ExternalInput").ap()
    out_ext = nc.dram_tensor("out", [c.S, c.MYO], F32, kind="ExternalOutput").ap()

    xn_scr = nc.dram_tensor("xn_scr", [c.S, c.D], BF16)
    qt_scr = nc.dram_tensor("qt_scr", [c.QH, 128, c.S], BF16)
    cc_in = nc.dram_tensor("cc_in", [c.QH * 128, c.S], BF16)
    cc_out = nc.dram_tensor("cc_out", [c.KVH, 2 * c.GRP * 128, c.S], BF16)
    ffn_scr = nc.dram_tensor("ffn_scr", [c.S, c.MYO], F32)
    wo_bf = nc.dram_tensor("wo_bf", [c.NOBW, c.NWCH, 128, c.OBW], BF16)

    with ExitStack() as top:
        sb = SB(nc)
        dma_sems = (["dcn", "dbs", "dwc", "dtb", "dwv", "dxt", "dam", "dac"]
                    + [f"dx{i}" for i in range(4)]
                    + [f"dat{i}" for i in range(4)]
                    + [f"{p}{i}" for p in ("dxn", "dff", "dou", "dq",
                                           "dqp", "dfl", "dxr", "dwb", "dwt", "dbf",
                                           "dwo") for i in range(2)])
        for s in ("pe", "vv", "ss", "gp", "cc") + tuple(dma_sems):
            sb.add_sem(top, s)

        ktab = top.enter_context(nc.sbuf_tensor("ktab", [128, c.KVH, c.S], BF16))
        vbuf = top.enter_context(nc.sbuf_tensor("vbuf", [128, c.NTOKB, c.KVH * 128], BF16))
        m0 = top.enter_context(nc.sbuf_tensor("m0", [128, GQ], BF16))
        m1 = top.enter_context(nc.sbuf_tensor("m1", [128, GQ], BF16))
        rt = top.enter_context(nc.sbuf_tensor("rtm", [128, 128], BF16))
        ones_f = top.enter_context(nc.sbuf_tensor("ones_f", [128, 128], F32))
        ones_bf = top.enter_context(nc.sbuf_tensor("ones_bf", [1, 128], BF16))
        ones_bb = top.enter_context(nc.sbuf_tensor("ones_bb", [128, 128], BF16))
        bias_bf = top.enter_context(nc.sbuf_tensor("bias_bf", [1, c.MYO], BF16))
        wer = top.enter_context(nc.sbuf_tensor("wer", [128, c.NERB], F32))
        cfs = top.enter_context(nc.sbuf_tensor("cfs", [128, 2 * c.NERB], F32))
        csts = top.enter_context(nc.sbuf_tensor("csts", [128, 3], F32))
        xat = top.enter_context(nc.sbuf_tensor("xat", [128, c.NERB, c.S], BF16))

        P0s = ExitStack()
        tabs = P0s.enter_context(nc.sbuf_tensor("tabs", [128, 4, c.TP], F32))
        xnt = P0s.enter_context(nc.sbuf_tensor("xnt", [128, c.ND, c.TP], BF16))
        wbuf = P0s.enter_context(nc.sbuf_tensor("wbuf", [128, 2, c.ND * 128], BF16))
        qb = P0s.enter_context(nc.sbuf_tensor("qb", [128, 2, c.TG], BF16))
        tt1 = P0s.enter_context(nc.sbuf_tensor("tt1", [128, c.TG], F32))
        tt2 = P0s.enter_context(nc.sbuf_tensor("tt2", [128, c.TG], F32))
        qto = P0s.enter_context(nc.sbuf_tensor("qto", [128, 2, c.TG], BF16))

        sb.dma("sync", m0[:, :], m0_in, "dcn")
        sb.dma("sync", m1[:, :], m1_in, "dcn")
        sb.dma("sync", rt[:, :], rt_in, "dcn")
        t_cf = sb.dma("sync", cfs[:, 0:c.NERB], cf_rep, "dcn")
        t_sc = sb.dma("sync", cfs[:, c.NERB:2 * c.NERB], sc_rep, "dcn")
        t_ones = sb.op("gpsimd", lambda E: E.memset(ones_f[:, :], 1.0), inc="gp", n=1)
        t_onbb = sb.op("gpsimd", lambda E: E.memset(ones_bb[:, :], 1.0), inc="gp", n=1)
        sb.op("gpsimd", lambda E: E.memset(csts[:, 0:1], float(c.eps)), inc="gp", n=1)
        sb.op("gpsimd", lambda E: E.memset(csts[:, 1:2], -0.5), inc="gp", n=1)
        t_csts = sb.op("gpsimd", lambda E: E.memset(csts[:, 2:3], -1.0), inc="gp", n=1)
        t_ones2 = sb.op("gpsimd", lambda E: E.memset(ones_bf[:, :], 1.0), inc="gp", n=1)
        t_bias = sb.dma("gpsimd", bias_bf[:, :], bias_in, "dbs")
        tab0_tok = [sb.dma("sync", tabs[:, 0, :], cosq_in[:, 0:c.TP], "dtb"),
                    sb.dma("sync", tabs[:, 1, :], sinq_in[:, 0:c.TP], "dtb"),
                    sb.dma("sync", tabs[:, 2, :], cosk_in[:, 0:c.TP], "dtb"),
                    sb.dma("sync", tabs[:, 3, :], sink_in[:, 0:c.TP], "dtb")]
        pre_w = [sb.dma("gpsimd",
                        wbuf[:, i, :].rearrange("p (c f) -> p c f", c=c.ND),
                        wk_t[i].rearrange("c p f -> p c f"), f"dwb{i}")
                 for i in range(min(2, c.KVH))]
        sb.waits("vector", [t_cf, t_sc])
        t_wer = sb.op("vector", lambda E: E.tensor_tensor(
            wer[:, :], cfs[:, 0:c.NERB], cfs[:, c.NERB:2 * c.NERB], op=ALU.mult), inc="vv")

        NCH = c.D // 512
        xnwrite = [None] * c.NTOKB
        with ExitStack() as P1:
            xtl = P1.enter_context(nc.sbuf_tensor("xtl", [128, 2, c.D], F32))
            xnb = P1.enter_context(nc.sbuf_tensor("xnb", [128, 2, c.D], BF16))
            stats = P1.enter_context(nc.sbuf_tensor("stats", [128, NCH * 6], F32))
            mvs = P1.enter_context(nc.sbuf_tensor("mvs", [128, 2 * c.NTOKB], F32))
            lnv = P1.enter_context(nc.sbuf_tensor("lnv", [128, 1], F32))
            rstds = P1.enter_context(nc.sbuf_tensor("rstds", [128, c.NTOKB], F32))
            mrs = P1.enter_context(nc.sbuf_tensor("mrs", [128, c.NTOKB], F32))

            free_x = [None, None]
            free_xn = [None, None]
            t_aggr = [None] * c.NTOKB
            t_rstd = [None] * c.NTOKB

            def emit_apply(i):
                sl = i % 2
                sb.waits("vector", [t_rstd[i], free_xn[sl]])
                t_mr = sb.op("vector", lambda E, i=i: E.tensor_tensor(
                    mrs[:, i:i + 1], mvs[:, 2 * i:2 * i + 1], rstds[:, i:i + 1],
                    op=ALU.mult), inc="vv")
                t_ap = sb.op("vector", lambda E, i=i, sl=sl: E.tensor_scalar(
                    xnb[:, sl, :], xtl[:, sl, :], rstds[:, i:i + 1], mrs[:, i:i + 1],
                    op0=ALU.mult, op1=ALU.subtract), inc="vv")
                free_x[sl] = t_ap
                sb.waits("sync", [t_ap])
                xnwrite[i] = sb.dma("sync", xn_scr.ap()[i * 128:(i + 1) * 128, :],
                                    xnb[:, sl, :], f"dxn{sl}")
                free_xn[sl] = xnwrite[i]

            xnt_toks = [None] * c.NPASS

            def emit_xnt(p):
                nw = [xnwrite[t] for t in range((p + 1) * (c.NTOKB // c.NPASS))]
                sb.waits("sync", nw[-2:] + ([xnt_toks[p - 1]] if p else []))
                tk = []
                for ch in range(c.ND):
                    srcx = xn_scr.ap()[p * c.TP:(p + 1) * c.TP, ch * 128:(ch + 1) * 128]
                    tk.append(sb.dma("sync", xnt[:, ch, :], srcx, "dxt", transpose=True))
                xnt_toks[p] = tk[-1]

            half_tiles = c.NTOKB // c.NPASS
            for i in range(c.NTOKB):
                sl = i % 2
                sb.waits("sync", [free_x[sl]])
                t_x = sb.dma("sync", xtl[:, sl, :], x_in[i * 128:(i + 1) * 128, :],
                             f"dx{sl}")
                if i >= 1:
                    emit_apply(i - 1)
                if i == half_tiles:
                    emit_xnt(0)
                sb.waits("vector", [t_x])
                for ch in range(NCH):
                    sb.op("vector", lambda E, sl=sl, ch=ch: E.bn_stats(
                        stats[:, ch * 6:(ch + 1) * 6], xtl[:, sl, ch * 512:(ch + 1) * 512]),
                        inc="vv")
                t_aggr[i] = sb.op("vector", lambda E, i=i: E.bn_aggr(
                    mvs[:, 2 * i:2 * i + 2], stats[:, :]), inc="vv")
                sb.waits("scalar", [t_aggr[i]] + ([t_csts] if i == 0 else []))
                sb.op("scalar", lambda E, i=i: E.activation(
                    lnv[:, :], mvs[:, 2 * i + 1:2 * i + 2], AF.Ln, bias=csts[:, 0:1]),
                    inc="ss")
                t_rstd[i] = sb.op("scalar", lambda E, i=i: E.activation(
                    rstds[:, i:i + 1], lnv[:, :], AF.Exp, scale=csts[:, 1:2]), inc="ss")
            emit_apply(c.NTOKB - 1)

        p1_snap = sb.snap()
        q_dmas = []
        q_head_tok = {}
        with ExitStack() as P2:
            wvb = P2.enter_context(nc.sbuf_tensor("wvb", [128, c.ND, c.KVH * 128], BF16))
            atb = P2.enter_context(nc.sbuf_tensor("atb", [128, c.NERB, c.ND * 128], BF16))
            psQ = [P2.enter_context(nc.psum_tensor(f"psQ{i}", [128, c.TG], F32)) for i in range(2)]
            psR = [P2.enter_context(nc.psum_tensor(f"psR{i}", [128, c.TG], F32)) for i in range(2)]
            psV = [P2.enter_context(nc.psum_tensor(f"psV{i}", [128, c.TG], F32)) for i in range(2)]

            t_wv = None
            free_wslot = [None, None]
            free_qb = [None, None]
            free_psQ = [None, None]
            free_psR = [None, None]
            free_psV = [None, None]
            free_qto = [None, None]
            free_xnt = None
            free_t1 = None
            free_t2 = None
            free_tabs = None
            tile_idx = 0
            vtile_idx = 0
            widx = 0

            for p in range(c.NPASS):
                if p == 0:
                    t_tab = tab0_tok
                else:
                    sb.waits("sync", [free_tabs])
                    t_tab = [sb.dma("sync", tabs[:, 0, :], cosq_in[:, p * c.TP:(p + 1) * c.TP], "dtb"),
                             sb.dma("sync", tabs[:, 1, :], sinq_in[:, p * c.TP:(p + 1) * c.TP], "dtb"),
                             sb.dma("sync", tabs[:, 2, :], cosk_in[:, p * c.TP:(p + 1) * c.TP], "dtb"),
                             sb.dma("sync", tabs[:, 3, :], sink_in[:, p * c.TP:(p + 1) * c.TP], "dtb")]
                    sb.waits("sync", [free_xnt])
                    emit_xnt(p)
                t_xnt = xnt_toks[p]

                colbs = [("k", i) for i in range(c.KVH)] + [("q", i) for i in range(c.QH)]
                for (kind, hidx) in colbs:
                    wsl = widx % 2
                    csel = wq_t[hidx] if kind == "q" else wk_t[hidx]
                    if widx < len(pre_w):
                        t_w = pre_w[widx]
                    else:
                        sb.waits("gpsimd", [free_wslot[wsl]])
                        t_w = sb.dma("gpsimd",
                                     wbuf[:, wsl, :].rearrange("p (c f) -> p c f", c=c.ND),
                                     csel.rearrange("c p f -> p c f"), f"dwb{wsl}")
                    last_mm_colb = None
                    for tg in range(c.NTG):
                        sl = tile_idx % 2
                        tloc = tg * c.TG
                        sb.waits("tensor", [t_w, t_xnt, free_psQ[sl]])
                        t_proj = None
                        for ch in range(c.ND):
                            t_proj = sb.op("tensor", lambda E, sl=sl, wsl=wsl, ch=ch, tg=tg: E.matmul(
                                psQ[sl][:, :], wbuf[:, wsl, ch * 128:(ch + 1) * 128],
                                xnt[:, ch, tg * c.TG:(tg + 1) * c.TG],
                                start=(ch == 0), stop=(ch == c.ND - 1)),
                                inc="pe" if ch == c.ND - 1 else None)
                        last_mm_colb = t_proj
                        sb.waits("scalar", [t_proj, free_qb[sl]])
                        t_qb = sb.op("scalar", lambda E, sl=sl: E.activation(
                            qb[:, sl, :], psQ[sl][:, :], AF.Copy), inc="ss")
                        free_psQ[sl] = t_qb
                        sb.waits("tensor", [t_qb, free_psR[sl]])
                        t_rot = sb.op("tensor", lambda E, sl=sl: E.matmul(
                            psR[sl][:, :], rt[:, :], qb[:, sl, :], start=True, stop=True),
                            inc="pe")
                        ti = 0 if kind == "q" else 2
                        sb.waits("vector", [t_qb, free_t1] + (t_tab if tile_idx == 0 or True else []))
                        tv1 = sb.op("vector", lambda E, sl=sl, ti=ti, tloc=tloc: E.tensor_tensor(
                            tt1[:, :], qb[:, sl, :], tabs[:, ti, tloc:tloc + c.TG], op=ALU.mult),
                            inc="vv")
                        free_qb[sl] = tv1
                        sb.waits("vector", [t_rot, free_t2])
                        tv2 = sb.op("vector", lambda E, sl=sl, ti=ti, tloc=tloc: E.tensor_tensor(
                            tt2[:, :], psR[sl][:, :], tabs[:, ti + 1, tloc:tloc + c.TG], op=ALU.mult),
                            inc="vv")
                        free_psR[sl] = tv2
                        toks0 = p * c.TP + tloc
                        if kind == "k":
                            t_cmb = sb.op("vector", lambda E, hidx=hidx, toks0=toks0: E.tensor_tensor(
                                ktab[:, hidx, toks0:toks0 + c.TG], tt1[:, :], tt2[:, :],
                                op=ALU.add), inc="vv")
                        else:
                            sb.waits("vector", [free_qto[sl]])
                            t_cmb = sb.op("vector", lambda E, sl=sl: E.tensor_tensor(
                                qto[:, sl, :], tt1[:, :], tt2[:, :], op=ALU.add), inc="vv")
                            sb.waits("scalar", [t_cmb])
                            t_dq = sb.dma("scalar", qt_scr.ap()[hidx, :, toks0:toks0 + c.TG],
                                          qto[:, sl, :], f"dq{sl}")
                            q_dmas.append(t_dq)
                            q_head_tok.setdefault(hidx, []).append(t_dq)
                            free_qto[sl] = t_dq
                        free_t1 = t_cmb
                        free_t2 = t_cmb
                        free_tabs = t_cmb
                        tile_idx += 1
                    free_wslot[wsl] = last_mm_colb
                    widx += 1

                if p == 0:
                    sb.barrier(p1_snap, engines=("gpsimd",), include=("vv", "dxn0", "dxn1"))
                    t_wv = sb.dma("gpsimd", wvb[:, :, :],
                                  wv_t.rearrange("c p f -> p c f"), "dwv")
                    t_at = sb.dma("gpsimd",
                                  atb[:, :, :].rearrange("p b (c f) -> p b c f", c=c.ND),
                                  at_t.rearrange("b c p f -> p b c f"), "dac")
                ntb = c.TP // 128
                t_pv = None
                for tb in range(ntb):
                    sl = vtile_idx % 2
                    gtokb = p * ntb + tb
                    sb.waits("tensor", [t_wv, t_xnt, free_psV[sl]])
                    for ch in range(c.ND):
                        t_pv = sb.op("tensor", lambda E, sl=sl, ch=ch, tb=tb: E.matmul(
                            psV[sl][:, 0:c.KVH * 128], xnt[:, ch, tb * 128:(tb + 1) * 128],
                            wvb[:, ch, :], start=(ch == 0), stop=(ch == c.ND - 1)),
                            inc="pe" if ch == c.ND - 1 else None)
                    sb.waits("vector", [t_pv])
                    t_ev = sb.op("vector", lambda E, sl=sl, gtokb=gtokb: E.tensor_copy(
                        vbuf[:, gtokb, :], psV[sl][:, 0:c.KVH * 128]), inc="vv")
                    free_psV[sl] = t_ev
                    vtile_idx += 1
                NSG = c.TP // c.SGW
                t_px = None
                for erb in range(c.NERB):
                    for sg in range(NSG):
                        sb.waits("tensor", [t_at, free_psV[0]] if t_px is None
                                 else [free_psV[0]])
                        for ch in range(c.ND):
                            t_px = sb.op("tensor", lambda E, erb=erb, sg=sg, ch=ch: E.matmul(
                                psV[0][:, 0:c.SGW], atb[:, erb, ch * 128:(ch + 1) * 128],
                                xnt[:, ch, sg * c.SGW:(sg + 1) * c.SGW],
                                start=(ch == 0), stop=(ch == c.ND - 1)),
                                inc="pe" if ch == c.ND - 1 else None)
                        sb.waits("vector", [t_px, t_wer])
                        col0 = p * c.TP + sg * c.SGW
                        t_s2 = sb.op("vector", lambda E, erb=erb, col0=col0: E.tensor_scalar(
                            xat[:, erb, col0:col0 + c.SGW], psV[0][:, 0:c.SGW],
                            wer[:, erb:erb + 1], None, op0=ALU.mult), inc="vv")
                        free_psV[0] = t_s2
                        t_xa_last = t_s2
                free_xnt = t_px
        P0s.close()

        p2_snap = sb.snap()
        NH = GQ // 512
        P34 = ExitStack()
        xnm = P34.enter_context(nc.sbuf_tensor("xnm", [128, c.ND, c.TPF], BF16))
        with ExitStack() as P3:
            qpack = P3.enter_context(nc.sbuf_tensor("qpack", [128, 2, GQ], BF16))
            expb = P3.enter_context(nc.sbuf_tensor("expb", [128, 4, GQ], BF16))
            expm = P3.enter_context(nc.sbuf_tensor("expm", [128, 4, GQ], BF16))
            acc4 = P3.enter_context(nc.sbuf_tensor("acc4", [128, GQ], F32))
            acc4b = P3.enter_context(nc.sbuf_tensor("acc4b", [128, GQ], BF16))
            lnd = P3.enter_context(nc.sbuf_tensor("lnd", [128, GQ], F32))
            invd = P3.enter_context(nc.sbuf_tensor("invd", [128, GQ], F32))
            atile = P3.enter_context(nc.sbuf_tensor("atile", [128, 4, GQ], BF16))
            wtb = P3.enter_context(nc.sbuf_tensor("wtb", [128, 2, c.ND * c.OBF], BF16))
            bftb = P3.enter_context(nc.sbuf_tensor("bftb", [128, 2, c.NERB * c.OBF], BF16))
            ftl = P3.enter_context(nc.sbuf_tensor("ftl", [128, 2, c.OBF], F32))
            psL = [P3.enter_context(nc.psum_tensor(f"psL{i}", [128, GQ], F32)) for i in range(2)]
            psA2 = P3.enter_context(nc.psum_tensor("psA2", [128, GQ], F32))
            psF = [P3.enter_context(nc.psum_tensor(f"psF{i}", [128, c.OBF], F32)) for i in range(2)]

            sb.barrier(p2_snap, exclude=("cc", "dwc"))

            xnm_pre = []
            for ch in range(c.ND):
                srcp = xn_scr.ap()[0:c.TPF, ch * 128:(ch + 1) * 128]
                xnm_pre.append(sb.dma("sync", xnm[:, ch, :], srcp, "dxt", transpose=True))
            t_wocast = sb.dma("gpsimd", wo_bf.ap(), wo_t, "dwc")

            free_qp = [None, None]
            free_psL = [None, None]
            free_expb = [[], [], [], []]
            free_expm = [[], [], [], []]
            free_psA2 = None
            free_acc = None
            free_accb = None
            free_atile = [None, None, None, None]
            free_invd = None
            ag_all = [None, None, None, None]
            qp_toks = {}

            def emit_qpack(it2):
                g2, j2 = divmod(it2, c.NQT)
                slq = it2 % 2
                srcq = qt_scr.ap()[g2 * c.GRP:(g2 + 1) * c.GRP, :,
                                   j2 * c.QT:(j2 + 1) * c.QT].rearrange("h p q -> p h q")
                need = []
                for hq in range(g2 * c.GRP, (g2 + 1) * c.GRP):
                    need += q_head_tok.get(hq, [])
                sb.waits("scalar", [free_qp[slq]] + need)
                qp_toks[it2] = sb.dma("scalar", qpack[:, slq, :], srcq, f"dqp{slq}")

            def ffn_emitter():
                CH_YIELD = 22
                nmm = 0

                def tick():
                    nonlocal nmm
                    nmm += 1
                    return nmm % CH_YIELD == 0

                free_xnm = None
                free_wt = [None, None]
                free_bft = [None, None]
                free_psF = [None, None]
                free_ftl = [None, None]
                fidx = 0
                t_f = None
                t_xnm = None
                for tgf in range(c.NTGF):
                    if tgf == 0:
                        t_xnm = xnm_pre[-1]
                    else:
                        sb.waits("sync", [free_xnm])
                        xnm_tok = []
                        for ch in range(c.ND):
                            src = xn_scr.ap()[tgf * c.TPF:(tgf + 1) * c.TPF,
                                              ch * 128:(ch + 1) * 128]
                            xnm_tok.append(sb.dma("sync", xnm[:, ch, :], src, "dxt",
                                                  transpose=True))
                        t_xnm = xnm_tok[-1]
                        for _ in range(16):
                            yield
                    for ob in range(c.NOBF):
                        wsl2 = ob % 2
                        sb.waits("gpsimd", [free_wt[wsl2]])
                        t_wt = sb.dma("gpsimd",
                                      wtb[:, wsl2, :].rearrange("p (c f) -> p c f", c=c.ND),
                                      wt_t[ob].rearrange("c p f -> p c f"), f"dwt{wsl2}")
                        sb.waits("gpsimd", [free_bft[wsl2]])
                        t_bft = sb.dma("gpsimd",
                                       bftb[:, wsl2, :].rearrange("p (b f) -> p b f", b=c.NERB),
                                       bft_t[ob].rearrange("b p f -> p b f"), f"dbf{wsl2}")
                        for tb in range(c.TPF // 128):
                            sl = fidx % 2
                            gtb = tgf * (c.TPF // 128) + tb
                            sb.waits("tensor", [t_wt, t_bft, t_xnm, free_psF[sl],
                                                t_ones2, t_bias])
                            for ch in range(c.ND):
                                sb.op("tensor", lambda E, sl=sl, ch=ch, tb=tb, wsl2=wsl2: E.matmul(
                                    psF[sl][:, :], xnm[:, ch, tb * 128:(tb + 1) * 128],
                                    wtb[:, wsl2, ch * c.OBF:(ch + 1) * c.OBF],
                                    start=(ch == 0), stop=False), inc=None)
                                if tick():
                                    yield
                            for erb in range(c.NERB):
                                sb.op("tensor", lambda E, sl=sl, erb=erb, gtb=gtb, wsl2=wsl2: E.matmul(
                                    psF[sl][:, :], xat[:, erb, gtb * 128:(gtb + 1) * 128],
                                    bftb[:, wsl2, erb * c.OBF:(erb + 1) * c.OBF],
                                    start=False, stop=False), inc=None)
                            t_f = sb.op("tensor", lambda E, sl=sl, ob=ob: E.matmul(
                                psF[sl][:, :], ones_bf[:, :],
                                bias_bf[:, ob * c.OBF:(ob + 1) * c.OBF],
                                start=False, stop=True), inc="pe")
                            sb.waits("vector", [t_f, free_ftl[sl]])
                            t_ev = sb.op("vector", lambda E, sl=sl: E.tensor_copy(
                                ftl[:, sl, :], psF[sl][:, :]), inc="vv")
                            free_psF[sl] = t_ev
                            sb.waits("sync", [t_ev])
                            dstf = ffn_scr.ap()[gtb * 128:(gtb + 1) * 128,
                                                ob * c.OBF:(ob + 1) * c.OBF]
                            t_fw = sb.dma("sync", dstf, ftl[:, sl, :], f"dff{sl}")
                            free_ftl[sl] = t_fw
                            ffn_w_all[sl] = t_fw
                            fidx += 1
                        free_wt[wsl2] = t_f
                        free_bft[wsl2] = t_f
                    free_xnm = t_f

            ffn_w_all = [None, None]
            ffn_gen = ffn_emitter()

            emit_qpack(0)
            it = 0
            for g in range(c.KVH):
                for j in range(c.NQT):
                    sl2 = it % 2
                    sl4a = it % 4
                    span = (j + 1) * (c.QT // 128)
                    if it + 1 < c.KVH * c.NQT:
                        emit_qpack(it + 1)
                    t_qp = qp_toks[it]

                    tok_acc = None
                    t_log = None
                    t_av = None
                    pend = []

                    def emit_av(w, g=g, span=span):
                        nonlocal t_av
                        kb, ksl4, esrc, t_src = w
                        sb.waits("tensor", [t_src] + ([free_psA2] if kb == 0 else []))
                        for hh in range(NH):
                            t_av = sb.op("tensor", lambda E, g=g, kb=kb, ksl4=ksl4, hh=hh,
                                         esrc=esrc, st=(kb == 0), sp=(kb == span - 1): E.matmul(
                                psA2[:, hh * 512:(hh + 1) * 512],
                                vbuf[:, kb, g * 128:(g + 1) * 128],
                                esrc[:, ksl4, hh * 512:(hh + 1) * 512],
                                start=st, stop=sp), inc="pe" if hh == NH - 1 else None)
                        if esrc is expm:
                            free_expm[ksl4] = free_expm[ksl4] + [t_av, tok_acc]
                        else:
                            free_expb[ksl4] = free_expb[ksl4] + [t_av, tok_acc]

                    for kb in range(span):
                        ksl = kb % 2
                        ksl4 = kb % 4
                        diag = (kb >= span - 2)
                        sb.waits("tensor", [t_qp, free_psL[ksl]])
                        for hh in range(NH):
                            t_log = sb.op("tensor", lambda E, g=g, kb=kb, ksl=ksl, hh=hh, sl2=sl2: E.matmul(
                                psL[ksl][:, hh * 512:(hh + 1) * 512],
                                ktab[:, g, kb * 128:(kb + 1) * 128],
                                qpack[:, sl2, hh * 512:(hh + 1) * 512],
                                start=True, stop=True), inc="pe" if hh == NH - 1 else None)
                        if len(pend) >= 3:
                            emit_av(pend.pop(0))
                        sb.waits("scalar", [t_log] + free_expb[ksl4])
                        t_exp = sb.op("scalar", lambda E, ksl4=ksl4, ksl=ksl: E.activation(
                            expb[:, ksl4, :], psL[ksl][:, :], AF.Exp), inc="ss")
                        free_expb[ksl4] = []
                        free_psL[ksl] = t_exp
                        esrc, t_src = expb, t_exp
                        if diag:
                            msk = m0 if kb == span - 2 else m1
                            sb.waits("vector", [t_exp] + free_expm[ksl4])
                            t_msk = sb.op("vector", lambda E, ksl4=ksl4, msk=msk: E.tensor_tensor(
                                expm[:, ksl4, :], expb[:, ksl4, :], msk[:, :], op=ALU.mult),
                                inc="vv")
                            free_expm[ksl4] = []
                            free_expb[ksl4] = [t_msk]
                            esrc, t_src = expm, t_msk
                        sb.waits("vector", [t_src] + ([free_acc] if kb == 0 else []))
                        if kb == 0:
                            tok_acc = sb.op("vector", lambda E, esrc=esrc, ksl4=ksl4: E.tensor_copy(
                                acc4[:, :], esrc[:, ksl4, :]), inc="vv")
                        else:
                            tok_acc = sb.op("vector", lambda E, esrc=esrc, ksl4=ksl4: E.tensor_tensor(
                                acc4[:, :], acc4[:, :], esrc[:, ksl4, :], op=ALU.add), inc="vv")
                        pend.append((kb, ksl4, esrc, t_src))
                        next(ffn_gen, None)
                    while pend:
                        emit_av(pend.pop(0))
                    tok_av_last = t_av
                    sb.waits("vector", [tok_acc, free_accb])
                    t_accb = sb.op("vector", lambda E: E.tensor_copy(
                        acc4b[:, :], acc4[:, :]), inc="vv")
                    free_acc = t_accb
                    dsl = span % 2
                    sb.waits("tensor", [t_accb, t_onbb, free_psL[dsl]])
                    t_dmm = None
                    for hh in range(NH):
                        t_dmm = sb.op("tensor", lambda E, hh=hh, dsl=dsl: E.matmul(
                            psL[dsl][:, hh * 512:(hh + 1) * 512], ones_bb[:, :],
                            acc4b[:, hh * 512:(hh + 1) * 512], start=True, stop=True),
                            inc="pe" if hh == NH - 1 else None)
                    free_accb = t_dmm
                    sb.waits("scalar", [t_dmm])
                    t_ln2 = sb.op("scalar", lambda E, dsl=dsl: E.activation(
                        lnd[:, :], psL[dsl][:, :], AF.Ln), inc="ss")
                    free_psL[dsl] = t_ln2
                    sb.waits("scalar", [free_invd])
                    t_inv = sb.op("scalar", lambda E: E.activation(
                        invd[:, :], lnd[:, :], AF.Exp, scale=csts[:, 2:3]), inc="ss")
                    sb.waits("vector", [t_inv, tok_av_last, free_atile[sl4a]])
                    t_nrm = sb.op("vector", lambda E, sl4a=sl4a: E.tensor_tensor(
                        atile[:, sl4a, :], psA2[:, :], invd[:, :], op=ALU.mult), inc="vv")
                    free_psA2 = t_nrm
                    free_invd = t_nrm
                    dst = cc_in.ap().rearrange("(h d) q -> d h q", d=128)[
                        :, g * c.GRP:(g + 1) * c.GRP, j * c.QT:(j + 1) * c.QT]
                    sb.waits("sync", [t_nrm])
                    t_wr = sb.dma("sync", dst, atile[:, sl4a, :], f"dat{sl4a}")
                    free_atile[sl4a] = t_wr
                    ag_all[sl4a] = t_wr
                    free_qp[sl2] = t_log
                    it += 1
                sb.waits("gpsimd", ag_all)
                rg = [[2 * i, 2 * i + 1] for i in range(c.n_cores // 2)]
                gr = c.GRP * 128
                sb.op("gpsimd", lambda E, g=g, gr=gr: E.collective_compute(
                    "AllGather", ALU.bypass, replica_groups=rg,
                    ins=[cc_in.ap()[g * gr:(g + 1) * gr, :]],
                    outs=[cc_out.ap()[g]]), inc="cc", n=1)

            for _ in ffn_gen:
                pass

        P34.close()

        p4_snap = sb.snap()
        with ExitStack() as P5:
            atm = P5.enter_context(nc.sbuf_tensor("atm", [128, c.NWCH, c.TPF], BF16))
            wob = P5.enter_context(nc.sbuf_tensor("wob", [128, 2, c.NWCH * c.OBW], BF16))
            ftl2 = P5.enter_context(nc.sbuf_tensor("ftl2", [128, 2, c.OBW], F32))
            xrl = P5.enter_context(nc.sbuf_tensor("xrl", [128, 2, c.OBW], F32))
            otl = P5.enter_context(nc.sbuf_tensor("otl", [128, 2, c.OBW], F32))
            psW = [P5.enter_context(nc.psum_tensor(f"psW{i}", [128, c.OBW], F32)) for i in range(2)]

            sb.barrier(p4_snap, exclude=("cc",))
            cc_tok = ("cc", sb.cnt["cc"])

            def cco_src(ch, tgf):
                half, hl = divmod(ch, c.QH)
                g2, hr = divmod(hl, c.GRP)
                row = (half * c.GRP + hr) * 128
                return cc_out.ap()[g2, row:row + 128, tgf * c.TPF:(tgf + 1) * c.TPF]

            free_atm = None
            free_wo = [None, None]
            free_psW = [None, None]
            free_ftl2 = [None, None]
            free_xrl = [None, None]
            free_otl = [None, None]
            widx2 = 0
            t_ow = None
            t_wmm = None
            for tgf in range(c.NTGF):
                sb.waits("sync", [cc_tok, free_atm])
                atm_tok = []
                for ch in range(c.NWCH):
                    atm_tok.append(sb.dma("sync", atm[:, ch, :], cco_src(ch, tgf), "dam"))
                t_atm = atm_tok[-1]
                for ob in range(c.NOBW):
                    wsl = (tgf * c.NOBW + ob) % 2
                    sb.waits("gpsimd", [t_wocast, free_wo[wsl]])
                    t_wo = sb.dma("gpsimd",
                                  wob[:, wsl, :].rearrange("p (c f) -> p c f", c=c.NWCH),
                                  wo_bf.ap()[ob].rearrange("c p f -> p c f"), f"dwo{wsl}")
                    for tb in range(c.TPF // 128):
                        sl = widx2 % 2
                        gtb = tgf * (c.TPF // 128) + tb
                        sb.waits("scalar", ffn_w_all if widx2 == 0 else [])
                        sb.waits("scalar", [free_ftl2[sl]])
                        srcf = ffn_scr.ap()[gtb * 128:(gtb + 1) * 128,
                                            ob * c.OBW:(ob + 1) * c.OBW]
                        t_lf = sb.dma("scalar", ftl2[:, sl, :], srcf, f"dfl{sl}")
                        sb.waits("scalar", [free_xrl[sl]])
                        srcx = x_oc[gtb * 128:(gtb + 1) * 128, ob * c.OBW:(ob + 1) * c.OBW]
                        t_lx = sb.dma("scalar", xrl[:, sl, :], srcx, f"dxr{sl}")
                        sb.waits("tensor", [t_wo, t_atm, free_psW[sl]])
                        for ch in range(c.NWCH):
                            t_wmm = sb.op("tensor", lambda E, wsl=wsl, sl=sl, ch=ch, tb=tb: E.matmul(
                                psW[sl][:, :], atm[:, ch, tb * 128:(tb + 1) * 128],
                                wob[:, wsl, ch * c.OBW:(ch + 1) * c.OBW],
                                start=(ch == 0), stop=(ch == c.NWCH - 1)),
                                inc="pe" if ch == c.NWCH - 1 else None)
                        sb.waits("vector", [t_wmm, t_lf, free_otl[sl]])
                        t_a1 = sb.op("vector", lambda E, sl=sl: E.tensor_tensor(
                            otl[:, sl, :], psW[sl][:, :], ftl2[:, sl, :], op=ALU.add), inc="vv")
                        free_psW[sl] = t_a1
                        free_ftl2[sl] = t_a1
                        sb.waits("vector", [t_lx])
                        t_a2 = sb.op("vector", lambda E, sl=sl: E.tensor_tensor(
                            otl[:, sl, :], otl[:, sl, :], xrl[:, sl, :], op=ALU.add), inc="vv")
                        free_xrl[sl] = t_a2
                        sb.waits("sync", [t_a2])
                        dsto = out_ext[gtb * 128:(gtb + 1) * 128,
                                       ob * c.OBW:(ob + 1) * c.OBW]
                        t_ow = sb.dma("sync", dsto, otl[:, sl, :], f"dou{sl}")
                        free_otl[sl] = t_ow
                        widx2 += 1
                    free_wo[wsl] = t_wmm
                free_atm = t_wmm

            sb.waits("sync", [free_otl[0], free_otl[1]])

        with nc.Block() as block:
            @block.tensor
            def _(E):
                for f in sb.streams["tensor"]:
                    f(E)

            @block.vector
            def _(E):
                for f in sb.streams["vector"]:
                    f(E)

            @block.scalar
            def _(E):
                for f in sb.streams["scalar"]:
                    f(E)

            @block.sync
            def _(E):
                for f in sb.streams["sync"]:
                    f(E)

            @block.gpsimd
            def _(E):
                for f in sb.streams["gpsimd"]:
                    f(E)

    return nc



def _rope_tables(S, HD, theta):
    inv = 1.0 / theta ** (np.arange(0, HD, 2, dtype=np.float64) / HD)
    t = np.arange(S, dtype=np.float64)
    freqs = np.outer(t, inv)
    emb = np.concatenate([freqs, freqs], axis=-1)
    return np.cos(emb).T.astype(np.float32), np.sin(emb).T.astype(np.float32)


def make_core_inputs(c, x_b, Wq_h, Wk_h, Wv_h, Wo_h, WT_h, AT, BfT_h, bias_h,
                     coeffs, scales, x_oc):
    cos, sin = _rope_tables(c.S, c.HD, c.rope_theta)
    s = 1.0 / math.sqrt(c.HD)
    m0 = (np.arange(128)[:, None] <= np.arange(c.QT)[None, :]).astype(np.float32)
    m1 = ((np.arange(128)[:, None] + 128) <= np.arange(c.QT)[None, :]).astype(np.float32)
    m0 = np.ascontiguousarray(np.tile(m0, (1, c.GRP))).astype(BF)
    m1 = np.ascontiguousarray(np.tile(m1, (1, c.GRP))).astype(BF)
    R = np.zeros((128, 128), np.float32)
    hh = 64
    for d in range(hh):
        R[d, d + hh] = -1.0
    for d in range(hh, 128):
        R[d, d - hh] = 1.0
    rt = np.ascontiguousarray(R.T).astype(BF)

    return {
        "x_b": np.ascontiguousarray(x_b),
        "x_oc": np.ascontiguousarray(x_oc),
        "wq_t": np.ascontiguousarray(
            Wq_h.reshape(c.ND, 128, c.QH, 128).transpose(2, 0, 1, 3)),
        "wk_t": np.ascontiguousarray(
            Wk_h.reshape(c.ND, 128, c.KVH, 128).transpose(2, 0, 1, 3)),
        "wv_t": np.ascontiguousarray(Wv_h.reshape(c.ND, 128, c.KVH * 128)),
        "wo_t": np.ascontiguousarray(
            Wo_h.reshape(c.NWCH, 128, c.NOBW, c.OBW).transpose(2, 0, 1, 3)),
        "wt_t": np.ascontiguousarray(
            WT_h.reshape(c.ND, 128, c.NOBF, c.OBF).transpose(2, 0, 1, 3)),
        "at_t": np.ascontiguousarray(
            AT.reshape(c.ND, 128, c.NERB, 128).transpose(2, 0, 1, 3)),
        "bft_t": np.ascontiguousarray(
            BfT_h.reshape(c.NERB, 128, c.NOBF, c.OBF).transpose(2, 0, 1, 3)),
        "bias_in": np.ascontiguousarray(bias_h.reshape(1, c.MYO)),
        "cf_rep": np.ascontiguousarray(
            np.repeat(coeffs, c.ER // len(coeffs)).reshape(c.NERB, 128).T),
        "sc_rep": np.ascontiguousarray(
            np.repeat(scales, c.ER // len(scales)).reshape(c.NERB, 128).T),
        "cosq": np.ascontiguousarray(cos * s),
        "sinq": np.ascontiguousarray(sin * s),
        "cosk": np.ascontiguousarray(cos),
        "sink": np.ascontiguousarray(sin),
        "mask0": m0,
        "mask1": m1,
        "rt": rt,
    }


_NC_CACHE = {}


def kernel(L0_attn_hidden_states, L0_attn_Wq, L0_attn_Wk, L0_attn_Wv, L0_attn_Wo,
           L0_ffn_W, L0_ffn_bias, L0_ffn_A, L0_ffn_Bf, L0_ffn_coeffs, L0_ffn_scales):
    x = np.asarray(L0_attn_hidden_states, dtype=np.float32)
    Wq = np.asarray(L0_attn_Wq, np.float32)
    Wk = np.asarray(L0_attn_Wk, np.float32)
    Wv = np.asarray(L0_attn_Wv, np.float32)
    Wo = np.asarray(L0_attn_Wo, np.float32)
    W = np.asarray(L0_ffn_W, np.float32)
    bias = np.asarray(L0_ffn_bias, np.float32)
    A = np.asarray(L0_ffn_A, np.float32)
    Bf = np.asarray(L0_ffn_Bf, np.float32)
    coeffs = np.asarray(L0_ffn_coeffs, np.float32)
    scales = np.asarray(L0_ffn_scales, np.float32)

    B, S, D = x.shape
    c = Cfg(S=S, D=D)
    WT = np.ascontiguousarray(W.T)
    AT = np.ascontiguousarray(A.reshape(c.ER, D).T)
    BfT = np.ascontiguousarray(Bf.transpose(0, 2, 1).reshape(c.ER, D))

    key = (S, D)
    if key not in _NC_CACHE:
        _NC_CACHE[key] = build_nc(c)
    nc = _NC_CACHE[key]

    in_maps = []
    for core in range(8):
        b, h = core // 2, core % 2
        qs = c.QH * 128
        ks = c.KVH * 128
        o0 = h * c.MYO
        im = make_core_inputs(
            c, x[b],
            Wq[:, h * qs:(h + 1) * qs],
            Wk[:, h * ks:(h + 1) * ks],
            Wv[:, h * ks:(h + 1) * ks],
            Wo[:, o0:o0 + c.MYO],
            WT[:, o0:o0 + c.MYO],
            AT,
            BfT[:, o0:o0 + c.MYO],
            bias[o0:o0 + c.MYO],
            coeffs, scales,
            x[b][:, o0:o0 + c.MYO],
        )
        in_maps.append(im)

    res = run_bass_kernel_spmd(nc, in_maps, core_ids=list(range(8)))
    out = np.empty((B, S, D), np.float32)
    for core in range(8):
        b, h = core // 2, core % 2
        out[b, :, h * c.MYO:(h + 1) * c.MYO] = res.results[core]["out"]
    return out


# version 29 (final)
# speedup vs baseline: 1.1594x; 1.1346x over previous
import math
from contextlib import ExitStack

import numpy as np
import ml_dtypes

import concourse.bass as bass
import concourse.mybir as mybir
from concourse.bass_utils import run_bass_kernel_spmd

F32 = mybir.dt.float32
BF16 = mybir.dt.bfloat16
AF = mybir.ActivationFunctionType
ALU = mybir.AluOpType
BF = ml_dtypes.bfloat16

ENGS = ("tensor", "vector", "scalar", "sync", "gpsimd")
ENG_SEM = {"tensor": "pe", "vector": "vv", "scalar": "ss"}


class Cfg:
    def __init__(self, S=2048, D=4096, QH=16, KVH=4, ER=256, n_cores=8, eps=1e-5,
                 rope_theta=10000.0):
        self.S = S
        self.D = D
        self.QH = QH
        self.KVH = KVH
        self.GRP = QH // KVH
        self.HD = 128
        self.ER = ER
        self.NERB = ER // 128
        self.ND = D // 128
        self.NTOKB = S // 128
        self.NPASS = 2
        self.TP = S // self.NPASS
        self.TG = min(512, self.TP)
        self.NTG = self.TP // self.TG
        self.QT = 256
        self.NQT = S // self.QT
        self.MYO = D // 2
        self.OBF = min(256, self.MYO)
        self.NOBF = self.MYO // self.OBF
        self.OBW = min(512, self.MYO)
        self.NOBW = self.MYO // self.OBW
        self.NWCH = 2 * QH
        self.NTGF = 2
        self.TPF = S // self.NTGF
        self.SGW = min(512, self.TPF)
        self.n_cores = n_cores
        self.eps = eps
        self.rope_theta = rope_theta


def merge_tok(a, b):
    if a is None:
        return b
    if b is None:
        return a
    if a[0] == b[0]:
        return a if a[1] >= b[1] else b
    return b


class SB:

    def __init__(self, nc):
        self.nc = nc
        self.streams = {e: [] for e in ENGS}
        self.cnt = {}
        self.sems = {}

    def add_sem(self, stack, name):
        self.sems[name] = stack.enter_context(self.nc.semaphore(name))
        self.cnt[name] = 0

    def wait(self, eng, tok):
        if tok is None:
            return
        name, val = tok
        if val <= 0:
            return
        h = self.sems[name]
        self.streams[eng].append(lambda E, h=h, v=val: E.wait_ge(h, v))

    def waits(self, eng, toks):
        best = {}
        for t in toks:
            if t is None:
                continue
            n, v = t
            if v > best.get(n, 0):
                best[n] = v
        for n, v in best.items():
            self.wait(eng, (n, v))

    def op(self, eng, f, inc=None, n=1):
        if eng in ("vector", "scalar"):
            own = ENG_SEM[eng]
            prev = self.cnt[own]
            if prev > 0:
                self.wait(eng, (own, prev))
        tok = None
        if inc is not None:
            self.cnt[inc] += n
            tok = (inc, self.cnt[inc])
            h = self.sems[inc]
            self.streams[eng].append(lambda E, f=f, h=h, n=n: f(E).then_inc(h, n))
        else:
            self.streams[eng].append(lambda E, f=f: f(E))
        return tok

    def dma(self, eng, out, in_, sem, **kw):
        self.cnt[sem] += 16
        tok = (sem, self.cnt[sem])
        h = self.sems[sem]
        self.streams[eng].append(
            lambda E, o=out, i=in_, h=h, kw=kw: E.dma_start(out=o, in_=i, **kw).then_inc(h, 16))
        return tok

    def snap(self):
        return dict(self.cnt)

    def barrier(self, snapshot, engines=ENGS, exclude=("cc",), include=None):
        for e in engines:
            for n, v in snapshot.items():
                if include is not None and n not in include:
                    continue
                if include is None and n in exclude:
                    continue
                if v > 0:
                    self.wait(e, (n, v))


def build_nc(cfg, debug=False):
    c = cfg
    nc = bass.Bass("TRN2", target_bir_lowering=False, debug=debug)

    x_in = nc.dram_tensor("x_b", [c.S, c.D], F32, kind="ExternalInput").ap()
    x_oc = nc.dram_tensor("x_oc", [c.S, c.MYO], F32, kind="ExternalInput").ap()
    wq_t = nc.dram_tensor("wq_t", [c.QH, c.ND, 128, 128], F32, kind="ExternalInput").ap()
    wk_t = nc.dram_tensor("wk_t", [c.KVH, c.ND, 128, 128], F32, kind="ExternalInput").ap()
    wv_t = nc.dram_tensor("wv_t", [c.ND, 128, c.KVH * 128], F32, kind="ExternalInput").ap()
    wo_t = nc.dram_tensor("wo_t", [c.NOBW, c.NWCH, 128, c.OBW], F32, kind="ExternalInput").ap()
    wt_t = nc.dram_tensor("wt_t", [c.NOBF, c.ND, 128, c.OBF], F32, kind="ExternalInput").ap()
    at_t = nc.dram_tensor("at_t", [c.NERB, c.ND, 128, 128], F32, kind="ExternalInput").ap()
    bft_t = nc.dram_tensor("bft_t", [c.NOBF, c.NERB, 128, c.OBF], F32, kind="ExternalInput").ap()
    bias_in = nc.dram_tensor("bias_in", [1, c.MYO], F32, kind="ExternalInput").ap()
    cf_rep = nc.dram_tensor("cf_rep", [128, c.NERB], F32, kind="ExternalInput").ap()
    sc_rep = nc.dram_tensor("sc_rep", [128, c.NERB], F32, kind="ExternalInput").ap()
    cosq_in = nc.dram_tensor("cosq", [128, c.S], F32, kind="ExternalInput").ap()
    sinq_in = nc.dram_tensor("sinq", [128, c.S], F32, kind="ExternalInput").ap()
    cosk_in = nc.dram_tensor("cosk", [128, c.S], F32, kind="ExternalInput").ap()
    sink_in = nc.dram_tensor("sink", [128, c.S], F32, kind="ExternalInput").ap()
    GQ = c.GRP * c.QT
    m0_in = nc.dram_tensor("mask0", [128, GQ], BF16, kind="ExternalInput").ap()
    m1_in = nc.dram_tensor("mask1", [128, GQ], BF16, kind="ExternalInput").ap()
    rt_in = nc.dram_tensor("rt", [128, 128], BF16, kind="ExternalInput").ap()
    out_ext = nc.dram_tensor("out", [c.S, c.MYO], F32, kind="ExternalOutput").ap()

    xn_scr = nc.dram_tensor("xn_scr", [c.S, c.D], BF16)
    qt_scr = nc.dram_tensor("qt_scr", [c.QH, 128, c.S], BF16)
    cc_in = nc.dram_tensor("cc_in", [c.QH * 128, c.S], BF16)
    cc_out = nc.dram_tensor("cc_out", [c.KVH, 2 * c.GRP * 128, c.S], BF16)
    ffn_scr = nc.dram_tensor("ffn_scr", [c.S, c.MYO], F32)
    wo_bf = nc.dram_tensor("wo_bf", [c.NOBW, c.NWCH, 128, c.OBW], BF16)

    with ExitStack() as top:
        sb = SB(nc)
        dma_sems = (["dcn", "dbs", "dwc", "dtb", "dwv", "dxt", "dam", "dac"]
                    + [f"dx{i}" for i in range(4)]
                    + [f"dat{i}" for i in range(4)]
                    + [f"{p}{i}" for p in ("dxn", "dff", "dou", "dq",
                                           "dqp", "dfl", "dxr", "dwb", "dwt", "dbf",
                                           "dwo") for i in range(2)])
        for s in ("pe", "vv", "ss", "gp", "cc") + tuple(dma_sems):
            sb.add_sem(top, s)

        ktab = top.enter_context(nc.sbuf_tensor("ktab", [128, c.KVH, c.S], BF16))
        vbuf = top.enter_context(nc.sbuf_tensor("vbuf", [128, c.NTOKB, c.KVH * 128], BF16))
        m0 = top.enter_context(nc.sbuf_tensor("m0", [128, GQ], BF16))
        m1 = top.enter_context(nc.sbuf_tensor("m1", [128, GQ], BF16))
        rt = top.enter_context(nc.sbuf_tensor("rtm", [128, 128], BF16))
        ones_f = top.enter_context(nc.sbuf_tensor("ones_f", [128, 128], F32))
        ones_bf = top.enter_context(nc.sbuf_tensor("ones_bf", [1, 128], BF16))
        ones_bb = top.enter_context(nc.sbuf_tensor("ones_bb", [128, 128], BF16))
        bias_bf = top.enter_context(nc.sbuf_tensor("bias_bf", [1, c.MYO], BF16))
        wer = top.enter_context(nc.sbuf_tensor("wer", [128, c.NERB], F32))
        cfs = top.enter_context(nc.sbuf_tensor("cfs", [128, 2 * c.NERB], F32))
        csts = top.enter_context(nc.sbuf_tensor("csts", [128, 3], F32))
        xat = top.enter_context(nc.sbuf_tensor("xat", [128, c.NERB, c.S], BF16))

        P0s = ExitStack()
        tabs = P0s.enter_context(nc.sbuf_tensor("tabs", [128, 4, c.TP], F32))
        xnt = P0s.enter_context(nc.sbuf_tensor("xnt", [128, c.ND, c.TP], BF16))
        wbuf = P0s.enter_context(nc.sbuf_tensor("wbuf", [128, 2, c.ND * 128], BF16))
        qb = P0s.enter_context(nc.sbuf_tensor("qb", [128, 2, c.TG], BF16))
        tt1 = P0s.enter_context(nc.sbuf_tensor("tt1", [128, c.TG], F32))
        tt2 = P0s.enter_context(nc.sbuf_tensor("tt2", [128, c.TG], F32))
        qto = P0s.enter_context(nc.sbuf_tensor("qto", [128, 2, c.TG], BF16))

        sb.dma("sync", m0[:, :], m0_in, "dcn")
        sb.dma("sync", m1[:, :], m1_in, "dcn")
        sb.dma("sync", rt[:, :], rt_in, "dcn")
        t_cf = sb.dma("sync", cfs[:, 0:c.NERB], cf_rep, "dcn")
        t_sc = sb.dma("sync", cfs[:, c.NERB:2 * c.NERB], sc_rep, "dcn")
        t_ones = sb.op("gpsimd", lambda E: E.memset(ones_f[:, :], 1.0), inc="gp", n=1)
        t_onbb = sb.op("gpsimd", lambda E: E.memset(ones_bb[:, :], 1.0), inc="gp", n=1)
        sb.op("gpsimd", lambda E: E.memset(csts[:, 0:1], float(c.eps)), inc="gp", n=1)
        sb.op("gpsimd", lambda E: E.memset(csts[:, 1:2], -0.5), inc="gp", n=1)
        t_csts = sb.op("gpsimd", lambda E: E.memset(csts[:, 2:3], -1.0), inc="gp", n=1)
        t_ones2 = sb.op("gpsimd", lambda E: E.memset(ones_bf[:, :], 1.0), inc="gp", n=1)
        t_bias = sb.dma("gpsimd", bias_bf[:, :], bias_in, "dbs")
        tab0_tok = [sb.dma("sync", tabs[:, 0, :], cosq_in[:, 0:c.TP], "dtb"),
                    sb.dma("sync", tabs[:, 1, :], sinq_in[:, 0:c.TP], "dtb"),
                    sb.dma("sync", tabs[:, 2, :], cosk_in[:, 0:c.TP], "dtb"),
                    sb.dma("sync", tabs[:, 3, :], sink_in[:, 0:c.TP], "dtb")]
        pre_w = [sb.dma("gpsimd",
                        wbuf[:, i, :].rearrange("p (c f) -> p c f", c=c.ND),
                        wk_t[i].rearrange("c p f -> p c f"), f"dwb{i}")
                 for i in range(min(2, c.KVH))]
        sb.waits("vector", [t_cf, t_sc])
        t_wer = sb.op("vector", lambda E: E.tensor_tensor(
            wer[:, :], cfs[:, 0:c.NERB], cfs[:, c.NERB:2 * c.NERB], op=ALU.mult), inc="vv")

        NCH = c.D // 512
        xnwrite = [None] * c.NTOKB
        with ExitStack() as P1:
            xtl = P1.enter_context(nc.sbuf_tensor("xtl", [128, 2, c.D], F32))
            xnb = P1.enter_context(nc.sbuf_tensor("xnb", [128, 2, c.D], BF16))
            stats = P1.enter_context(nc.sbuf_tensor("stats", [128, NCH * 6], F32))
            mvs = P1.enter_context(nc.sbuf_tensor("mvs", [128, 2 * c.NTOKB], F32))
            lnv = P1.enter_context(nc.sbuf_tensor("lnv", [128, 1], F32))
            rstds = P1.enter_context(nc.sbuf_tensor("rstds", [128, c.NTOKB], F32))
            mrs = P1.enter_context(nc.sbuf_tensor("mrs", [128, c.NTOKB], F32))

            free_x = [None, None]
            free_xn = [None, None]
            t_aggr = [None] * c.NTOKB
            t_rstd = [None] * c.NTOKB

            def emit_apply(i):
                sl = i % 2
                sb.waits("vector", [t_rstd[i], free_xn[sl]])
                t_mr = sb.op("vector", lambda E, i=i: E.tensor_tensor(
                    mrs[:, i:i + 1], mvs[:, 2 * i:2 * i + 1], rstds[:, i:i + 1],
                    op=ALU.mult), inc="vv")
                t_ap = sb.op("vector", lambda E, i=i, sl=sl: E.tensor_scalar(
                    xnb[:, sl, :], xtl[:, sl, :], rstds[:, i:i + 1], mrs[:, i:i + 1],
                    op0=ALU.mult, op1=ALU.subtract), inc="vv")
                free_x[sl] = t_ap
                sb.waits("sync", [t_ap])
                xnwrite[i] = sb.dma("sync", xn_scr.ap()[i * 128:(i + 1) * 128, :],
                                    xnb[:, sl, :], f"dxn{sl}")
                free_xn[sl] = xnwrite[i]

            xnt_toks = [None] * c.NPASS

            def emit_xnt(p):
                nw = [xnwrite[t] for t in range((p + 1) * (c.NTOKB // c.NPASS))]
                sb.waits("sync", nw[-2:] + ([xnt_toks[p - 1]] if p else []))
                tk = []
                for ch in range(c.ND):
                    srcx = xn_scr.ap()[p * c.TP:(p + 1) * c.TP, ch * 128:(ch + 1) * 128]
                    tk.append(sb.dma("sync", xnt[:, ch, :], srcx, "dxt", transpose=True))
                xnt_toks[p] = tk[-1]

            half_tiles = c.NTOKB // c.NPASS
            for i in range(c.NTOKB):
                sl = i % 2
                sb.waits("sync", [free_x[sl]])
                t_x = sb.dma("sync", xtl[:, sl, :], x_in[i * 128:(i + 1) * 128, :],
                             f"dx{sl}")
                if i >= 1:
                    emit_apply(i - 1)
                if i == half_tiles:
                    emit_xnt(0)
                sb.waits("vector", [t_x])
                for ch in range(NCH):
                    sb.op("vector", lambda E, sl=sl, ch=ch: E.bn_stats(
                        stats[:, ch * 6:(ch + 1) * 6], xtl[:, sl, ch * 512:(ch + 1) * 512]),
                        inc="vv")
                t_aggr[i] = sb.op("vector", lambda E, i=i: E.bn_aggr(
                    mvs[:, 2 * i:2 * i + 2], stats[:, :]), inc="vv")
                sb.waits("scalar", [t_aggr[i]] + ([t_csts] if i == 0 else []))
                sb.op("scalar", lambda E, i=i: E.activation(
                    lnv[:, :], mvs[:, 2 * i + 1:2 * i + 2], AF.Ln, bias=csts[:, 0:1]),
                    inc="ss")
                t_rstd[i] = sb.op("scalar", lambda E, i=i: E.activation(
                    rstds[:, i:i + 1], lnv[:, :], AF.Exp, scale=csts[:, 1:2]), inc="ss")
            emit_apply(c.NTOKB - 1)

        p1_snap = sb.snap()
        q_dmas = []
        q_head_tok = {}
        with ExitStack() as P2:
            wvb = P2.enter_context(nc.sbuf_tensor("wvb", [128, c.ND, c.KVH * 128], BF16))
            atb = P2.enter_context(nc.sbuf_tensor("atb", [128, c.NERB, c.ND * 128], BF16))
            psQ = [P2.enter_context(nc.psum_tensor(f"psQ{i}", [128, c.TG], F32)) for i in range(2)]
            psR = [P2.enter_context(nc.psum_tensor(f"psR{i}", [128, c.TG], F32)) for i in range(2)]
            psV = [P2.enter_context(nc.psum_tensor(f"psV{i}", [128, c.TG], F32)) for i in range(2)]

            t_wv = None
            free_wslot = [None, None]
            free_qb = [None, None]
            free_psQ = [None, None]
            free_psR = [None, None]
            free_psV = [None, None]
            free_qto = [None, None]
            free_xnt = None
            free_t1 = None
            free_t2 = None
            free_tabs = None
            tile_idx = 0
            vtile_idx = 0
            widx = 0

            for p in range(c.NPASS):
                if p == 0:
                    t_tab = tab0_tok
                else:
                    sb.waits("sync", [free_tabs])
                    t_tab = [sb.dma("sync", tabs[:, 0, :], cosq_in[:, p * c.TP:(p + 1) * c.TP], "dtb"),
                             sb.dma("sync", tabs[:, 1, :], sinq_in[:, p * c.TP:(p + 1) * c.TP], "dtb"),
                             sb.dma("sync", tabs[:, 2, :], cosk_in[:, p * c.TP:(p + 1) * c.TP], "dtb"),
                             sb.dma("sync", tabs[:, 3, :], sink_in[:, p * c.TP:(p + 1) * c.TP], "dtb")]
                    sb.waits("sync", [free_xnt])
                    emit_xnt(p)
                t_xnt = xnt_toks[p]

                colbs = [("k", i) for i in range(c.KVH)] + [("q", i) for i in range(c.QH)]
                for (kind, hidx) in colbs:
                    wsl = widx % 2
                    csel = wq_t[hidx] if kind == "q" else wk_t[hidx]
                    if widx < len(pre_w):
                        t_w = pre_w[widx]
                    else:
                        sb.waits("gpsimd", [free_wslot[wsl]])
                        t_w = sb.dma("gpsimd",
                                     wbuf[:, wsl, :].rearrange("p (c f) -> p c f", c=c.ND),
                                     csel.rearrange("c p f -> p c f"), f"dwb{wsl}")
                    last_mm_colb = None
                    for tg in range(c.NTG):
                        sl = tile_idx % 2
                        tloc = tg * c.TG
                        sb.waits("tensor", [t_w, t_xnt, free_psQ[sl]])
                        t_proj = None
                        for ch in range(c.ND):
                            t_proj = sb.op("tensor", lambda E, sl=sl, wsl=wsl, ch=ch, tg=tg: E.matmul(
                                psQ[sl][:, :], wbuf[:, wsl, ch * 128:(ch + 1) * 128],
                                xnt[:, ch, tg * c.TG:(tg + 1) * c.TG],
                                start=(ch == 0), stop=(ch == c.ND - 1)),
                                inc="pe" if ch == c.ND - 1 else None)
                        last_mm_colb = t_proj
                        sb.waits("scalar", [t_proj, free_qb[sl]])
                        t_qb = sb.op("scalar", lambda E, sl=sl: E.activation(
                            qb[:, sl, :], psQ[sl][:, :], AF.Copy), inc="ss")
                        free_psQ[sl] = t_qb
                        sb.waits("tensor", [t_qb, free_psR[sl]])
                        t_rot = sb.op("tensor", lambda E, sl=sl: E.matmul(
                            psR[sl][:, :], rt[:, :], qb[:, sl, :], start=True, stop=True),
                            inc="pe")
                        ti = 0 if kind == "q" else 2
                        sb.waits("vector", [t_qb, free_t1] + (t_tab if tile_idx == 0 or True else []))
                        tv1 = sb.op("vector", lambda E, sl=sl, ti=ti, tloc=tloc: E.tensor_tensor(
                            tt1[:, :], qb[:, sl, :], tabs[:, ti, tloc:tloc + c.TG], op=ALU.mult),
                            inc="vv")
                        free_qb[sl] = tv1
                        sb.waits("vector", [t_rot, free_t2])
                        tv2 = sb.op("vector", lambda E, sl=sl, ti=ti, tloc=tloc: E.tensor_tensor(
                            tt2[:, :], psR[sl][:, :], tabs[:, ti + 1, tloc:tloc + c.TG], op=ALU.mult),
                            inc="vv")
                        free_psR[sl] = tv2
                        toks0 = p * c.TP + tloc
                        if kind == "k":
                            t_cmb = sb.op("vector", lambda E, hidx=hidx, toks0=toks0: E.tensor_tensor(
                                ktab[:, hidx, toks0:toks0 + c.TG], tt1[:, :], tt2[:, :],
                                op=ALU.add), inc="vv")
                        else:
                            sb.waits("vector", [free_qto[sl]])
                            t_cmb = sb.op("vector", lambda E, sl=sl: E.tensor_tensor(
                                qto[:, sl, :], tt1[:, :], tt2[:, :], op=ALU.add), inc="vv")
                            sb.waits("scalar", [t_cmb])
                            t_dq = sb.dma("scalar", qt_scr.ap()[hidx, :, toks0:toks0 + c.TG],
                                          qto[:, sl, :], f"dq{sl}")
                            q_dmas.append(t_dq)
                            q_head_tok.setdefault(hidx, []).append(t_dq)
                            free_qto[sl] = t_dq
                        free_t1 = t_cmb
                        free_t2 = t_cmb
                        free_tabs = t_cmb
                        tile_idx += 1
                    free_wslot[wsl] = last_mm_colb
                    widx += 1

                if p == 0:
                    sb.barrier(p1_snap, engines=("gpsimd",), include=("vv", "dxn0", "dxn1"))
                    t_wv = sb.dma("gpsimd", wvb[:, :, :],
                                  wv_t.rearrange("c p f -> p c f"), "dwv")
                    t_at = sb.dma("gpsimd",
                                  atb[:, :, :].rearrange("p b (c f) -> p b c f", c=c.ND),
                                  at_t.rearrange("b c p f -> p b c f"), "dac")
                ntb = c.TP // 128
                t_pv = None
                for tb in range(ntb):
                    sl = vtile_idx % 2
                    gtokb = p * ntb + tb
                    sb.waits("tensor", [t_wv, t_xnt, free_psV[sl]])
                    for ch in range(c.ND):
                        t_pv = sb.op("tensor", lambda E, sl=sl, ch=ch, tb=tb: E.matmul(
                            psV[sl][:, 0:c.KVH * 128], xnt[:, ch, tb * 128:(tb + 1) * 128],
                            wvb[:, ch, :], start=(ch == 0), stop=(ch == c.ND - 1)),
                            inc="pe" if ch == c.ND - 1 else None)
                    sb.waits("vector", [t_pv])
                    t_ev = sb.op("vector", lambda E, sl=sl, gtokb=gtokb: E.tensor_copy(
                        vbuf[:, gtokb, :], psV[sl][:, 0:c.KVH * 128]), inc="vv")
                    free_psV[sl] = t_ev
                    vtile_idx += 1
                NSG = c.TP // c.SGW
                t_px = None
                for erb in range(c.NERB):
                    for sg in range(NSG):
                        sb.waits("tensor", [t_at, free_psV[0]] if t_px is None
                                 else [free_psV[0]])
                        for ch in range(c.ND):
                            t_px = sb.op("tensor", lambda E, erb=erb, sg=sg, ch=ch: E.matmul(
                                psV[0][:, 0:c.SGW], atb[:, erb, ch * 128:(ch + 1) * 128],
                                xnt[:, ch, sg * c.SGW:(sg + 1) * c.SGW],
                                start=(ch == 0), stop=(ch == c.ND - 1)),
                                inc="pe" if ch == c.ND - 1 else None)
                        sb.waits("vector", [t_px, t_wer])
                        col0 = p * c.TP + sg * c.SGW
                        t_s2 = sb.op("vector", lambda E, erb=erb, col0=col0: E.tensor_scalar(
                            xat[:, erb, col0:col0 + c.SGW], psV[0][:, 0:c.SGW],
                            wer[:, erb:erb + 1], None, op0=ALU.mult), inc="vv")
                        free_psV[0] = t_s2
                        t_xa_last = t_s2
                free_xnt = t_px
        P0s.close()

        p2_snap = sb.snap()
        NH = GQ // 512
        P34 = ExitStack()
        xnm = P34.enter_context(nc.sbuf_tensor("xnm", [128, c.ND, c.TPF], BF16))
        with ExitStack() as P3:
            qpack = P3.enter_context(nc.sbuf_tensor("qpack", [128, 2, GQ], BF16))
            expb = P3.enter_context(nc.sbuf_tensor("expb", [128, 4, GQ], BF16))
            expm = P3.enter_context(nc.sbuf_tensor("expm", [128, 4, GQ], BF16))
            acc4 = P3.enter_context(nc.sbuf_tensor("acc4", [128, GQ], F32))
            acc4b = P3.enter_context(nc.sbuf_tensor("acc4b", [128, GQ], BF16))
            lnd = P3.enter_context(nc.sbuf_tensor("lnd", [128, GQ], F32))
            invd = P3.enter_context(nc.sbuf_tensor("invd", [128, GQ], F32))
            atile = P3.enter_context(nc.sbuf_tensor("atile", [128, 4, GQ], BF16))
            wtb = P3.enter_context(nc.sbuf_tensor("wtb", [128, 2, c.ND * c.OBF], BF16))
            bftb = P3.enter_context(nc.sbuf_tensor("bftb", [128, 2, c.NERB * c.OBF], BF16))
            ftl = P3.enter_context(nc.sbuf_tensor("ftl", [128, 2, c.OBF], F32))
            psL = [P3.enter_context(nc.psum_tensor(f"psL{i}", [128, GQ], F32)) for i in range(2)]
            psA2 = P3.enter_context(nc.psum_tensor("psA2", [128, GQ], F32))
            psF = [P3.enter_context(nc.psum_tensor(f"psF{i}", [128, c.OBF], F32)) for i in range(2)]

            sb.barrier(p2_snap, exclude=("cc", "dwc"))

            xnm_pre = []
            for ch in range(c.ND):
                srcp = xn_scr.ap()[0:c.TPF, ch * 128:(ch + 1) * 128]
                xnm_pre.append(sb.dma("sync", xnm[:, ch, :], srcp, "dxt", transpose=True))
            t_wocast = sb.dma("gpsimd", wo_bf.ap(), wo_t, "dwc")

            free_qp = [None, None]
            free_psL = [None, None]
            free_expb = [[], [], [], []]
            free_expm = [[], [], [], []]
            free_psA2 = None
            free_acc = None
            free_accb = None
            free_atile = [None, None, None, None]
            free_invd = None
            ag_all = [None, None, None, None]
            qp_toks = {}

            def emit_qpack(it2):
                g2, j2 = divmod(it2, c.NQT)
                slq = it2 % 2
                srcq = qt_scr.ap()[g2 * c.GRP:(g2 + 1) * c.GRP, :,
                                   j2 * c.QT:(j2 + 1) * c.QT].rearrange("h p q -> p h q")
                need = []
                for hq in range(g2 * c.GRP, (g2 + 1) * c.GRP):
                    need += q_head_tok.get(hq, [])
                sb.waits("scalar", [free_qp[slq]] + need)
                qp_toks[it2] = sb.dma("scalar", qpack[:, slq, :], srcq, f"dqp{slq}")

            def ffn_emitter():
                CH_YIELD = 16
                nmm = 0

                def tick():
                    nonlocal nmm
                    nmm += 1
                    return nmm % CH_YIELD == 0

                free_xnm = None
                free_wt = [None, None]
                free_bft = [None, None]
                free_psF = [None, None]
                free_ftl = [None, None]
                fidx = 0
                t_f = None
                t_xnm = None
                for tgf in range(c.NTGF):
                    if tgf == 0:
                        t_xnm = xnm_pre[-1]
                    else:
                        sb.waits("sync", [free_xnm])
                        xnm_tok = []
                        for ch in range(c.ND):
                            src = xn_scr.ap()[tgf * c.TPF:(tgf + 1) * c.TPF,
                                              ch * 128:(ch + 1) * 128]
                            xnm_tok.append(sb.dma("sync", xnm[:, ch, :], src, "dxt",
                                                  transpose=True))
                        t_xnm = xnm_tok[-1]
                        for _ in range(32):
                            yield
                    for ob in range(c.NOBF):
                        wsl2 = ob % 2
                        sb.waits("gpsimd", [free_wt[wsl2]])
                        t_wt = sb.dma("gpsimd",
                                      wtb[:, wsl2, :].rearrange("p (c f) -> p c f", c=c.ND),
                                      wt_t[ob].rearrange("c p f -> p c f"), f"dwt{wsl2}")
                        sb.waits("gpsimd", [free_bft[wsl2]])
                        t_bft = sb.dma("gpsimd",
                                       bftb[:, wsl2, :].rearrange("p (b f) -> p b f", b=c.NERB),
                                       bft_t[ob].rearrange("b p f -> p b f"), f"dbf{wsl2}")
                        for tb in range(c.TPF // 128):
                            sl = fidx % 2
                            gtb = tgf * (c.TPF // 128) + tb
                            sb.waits("tensor", [t_wt, t_bft, t_xnm, free_psF[sl],
                                                t_ones2, t_bias])
                            for ch in range(c.ND):
                                sb.op("tensor", lambda E, sl=sl, ch=ch, tb=tb, wsl2=wsl2: E.matmul(
                                    psF[sl][:, :], xnm[:, ch, tb * 128:(tb + 1) * 128],
                                    wtb[:, wsl2, ch * c.OBF:(ch + 1) * c.OBF],
                                    start=(ch == 0), stop=False), inc=None)
                                if tick():
                                    yield
                            for erb in range(c.NERB):
                                sb.op("tensor", lambda E, sl=sl, erb=erb, gtb=gtb, wsl2=wsl2: E.matmul(
                                    psF[sl][:, :], xat[:, erb, gtb * 128:(gtb + 1) * 128],
                                    bftb[:, wsl2, erb * c.OBF:(erb + 1) * c.OBF],
                                    start=False, stop=False), inc=None)
                            t_f = sb.op("tensor", lambda E, sl=sl, ob=ob: E.matmul(
                                psF[sl][:, :], ones_bf[:, :],
                                bias_bf[:, ob * c.OBF:(ob + 1) * c.OBF],
                                start=False, stop=True), inc="pe")
                            sb.waits("vector", [t_f, free_ftl[sl]])
                            t_ev = sb.op("vector", lambda E, sl=sl: E.tensor_copy(
                                ftl[:, sl, :], psF[sl][:, :]), inc="vv")
                            free_psF[sl] = t_ev
                            sb.waits("sync", [t_ev])
                            dstf = ffn_scr.ap()[gtb * 128:(gtb + 1) * 128,
                                                ob * c.OBF:(ob + 1) * c.OBF]
                            t_fw = sb.dma("sync", dstf, ftl[:, sl, :], f"dff{sl}")
                            free_ftl[sl] = t_fw
                            ffn_w_all[sl] = t_fw
                            fidx += 1
                        free_wt[wsl2] = t_f
                        free_bft[wsl2] = t_f
                    free_xnm = t_f

            ffn_w_all = [None, None]
            ffn_gen = ffn_emitter()

            emit_qpack(0)
            it = 0
            for g in range(c.KVH):
                for j in range(c.NQT):
                    sl2 = it % 2
                    sl4a = it % 4
                    span = (j + 1) * (c.QT // 128)
                    if it + 1 < c.KVH * c.NQT:
                        emit_qpack(it + 1)
                    t_qp = qp_toks[it]

                    tok_acc = None
                    t_log = None
                    t_av = None
                    pend = []

                    def emit_av(w, g=g, span=span):
                        nonlocal t_av
                        kb, ksl4, esrc, t_src = w
                        sb.waits("tensor", [t_src] + ([free_psA2] if kb == 0 else []))
                        for hh in range(NH):
                            t_av = sb.op("tensor", lambda E, g=g, kb=kb, ksl4=ksl4, hh=hh,
                                         esrc=esrc, st=(kb == 0), sp=(kb == span - 1): E.matmul(
                                psA2[:, hh * 512:(hh + 1) * 512],
                                vbuf[:, kb, g * 128:(g + 1) * 128],
                                esrc[:, ksl4, hh * 512:(hh + 1) * 512],
                                start=st, stop=sp), inc="pe" if hh == NH - 1 else None)
                        if esrc is expm:
                            free_expm[ksl4] = free_expm[ksl4] + [t_av, tok_acc]
                        else:
                            free_expb[ksl4] = free_expb[ksl4] + [t_av, tok_acc]

                    for kb in range(span):
                        ksl = kb % 2
                        ksl4 = kb % 4
                        diag = (kb >= span - 2)
                        sb.waits("tensor", [t_qp, free_psL[ksl]])
                        for hh in range(NH):
                            t_log = sb.op("tensor", lambda E, g=g, kb=kb, ksl=ksl, hh=hh, sl2=sl2: E.matmul(
                                psL[ksl][:, hh * 512:(hh + 1) * 512],
                                ktab[:, g, kb * 128:(kb + 1) * 128],
                                qpack[:, sl2, hh * 512:(hh + 1) * 512],
                                start=True, stop=True), inc="pe" if hh == NH - 1 else None)
                        if len(pend) >= 3:
                            emit_av(pend.pop(0))
                        sb.waits("scalar", [t_log] + free_expb[ksl4])
                        t_exp = sb.op("scalar", lambda E, ksl4=ksl4, ksl=ksl: E.activation(
                            expb[:, ksl4, :], psL[ksl][:, :], AF.Exp), inc="ss")
                        free_expb[ksl4] = []
                        free_psL[ksl] = t_exp
                        esrc, t_src = expb, t_exp
                        if diag:
                            msk = m0 if kb == span - 2 else m1
                            sb.waits("vector", [t_exp] + free_expm[ksl4])
                            t_msk = sb.op("vector", lambda E, ksl4=ksl4, msk=msk: E.tensor_tensor(
                                expm[:, ksl4, :], expb[:, ksl4, :], msk[:, :], op=ALU.mult),
                                inc="vv")
                            free_expm[ksl4] = []
                            free_expb[ksl4] = [t_msk]
                            esrc, t_src = expm, t_msk
                        sb.waits("vector", [t_src] + ([free_acc] if kb == 0 else []))
                        if kb == 0:
                            tok_acc = sb.op("vector", lambda E, esrc=esrc, ksl4=ksl4: E.tensor_copy(
                                acc4[:, :], esrc[:, ksl4, :]), inc="vv")
                        else:
                            tok_acc = sb.op("vector", lambda E, esrc=esrc, ksl4=ksl4: E.tensor_tensor(
                                acc4[:, :], acc4[:, :], esrc[:, ksl4, :], op=ALU.add), inc="vv")
                        pend.append((kb, ksl4, esrc, t_src))
                        next(ffn_gen, None)
                    while pend:
                        emit_av(pend.pop(0))
                    tok_av_last = t_av
                    sb.waits("vector", [tok_acc, free_accb])
                    t_accb = sb.op("vector", lambda E: E.tensor_copy(
                        acc4b[:, :], acc4[:, :]), inc="vv")
                    free_acc = t_accb
                    dsl = span % 2
                    sb.waits("tensor", [t_accb, t_onbb, free_psL[dsl]])
                    t_dmm = None
                    for hh in range(NH):
                        t_dmm = sb.op("tensor", lambda E, hh=hh, dsl=dsl: E.matmul(
                            psL[dsl][:, hh * 512:(hh + 1) * 512], ones_bb[:, :],
                            acc4b[:, hh * 512:(hh + 1) * 512], start=True, stop=True),
                            inc="pe" if hh == NH - 1 else None)
                    free_accb = t_dmm
                    sb.waits("scalar", [t_dmm])
                    t_ln2 = sb.op("scalar", lambda E, dsl=dsl: E.activation(
                        lnd[:, :], psL[dsl][:, :], AF.Ln), inc="ss")
                    free_psL[dsl] = t_ln2
                    sb.waits("scalar", [free_invd])
                    t_inv = sb.op("scalar", lambda E: E.activation(
                        invd[:, :], lnd[:, :], AF.Exp, scale=csts[:, 2:3]), inc="ss")
                    sb.waits("vector", [t_inv, tok_av_last, free_atile[sl4a]])
                    t_nrm = sb.op("vector", lambda E, sl4a=sl4a: E.tensor_tensor(
                        atile[:, sl4a, :], psA2[:, :], invd[:, :], op=ALU.mult), inc="vv")
                    free_psA2 = t_nrm
                    free_invd = t_nrm
                    dst = cc_in.ap().rearrange("(h d) q -> d h q", d=128)[
                        :, g * c.GRP:(g + 1) * c.GRP, j * c.QT:(j + 1) * c.QT]
                    sb.waits("sync", [t_nrm])
                    t_wr = sb.dma("sync", dst, atile[:, sl4a, :], f"dat{sl4a}")
                    free_atile[sl4a] = t_wr
                    ag_all[sl4a] = t_wr
                    free_qp[sl2] = t_log
                    it += 1
                sb.waits("gpsimd", ag_all)
                rg = [[2 * i, 2 * i + 1] for i in range(c.n_cores // 2)]
                gr = c.GRP * 128
                sb.op("gpsimd", lambda E, g=g, gr=gr: E.collective_compute(
                    "AllGather", ALU.bypass, replica_groups=rg,
                    ins=[cc_in.ap()[g * gr:(g + 1) * gr, :]],
                    outs=[cc_out.ap()[g]]), inc="cc", n=1)

            for _ in ffn_gen:
                pass

        P34.close()

        p4_snap = sb.snap()
        with ExitStack() as P5:
            atm = P5.enter_context(nc.sbuf_tensor("atm", [128, c.NWCH, c.TPF], BF16))
            wob = P5.enter_context(nc.sbuf_tensor("wob", [128, 2, c.NWCH * c.OBW], BF16))
            ftl2 = P5.enter_context(nc.sbuf_tensor("ftl2", [128, 2, c.OBW], F32))
            xrl = P5.enter_context(nc.sbuf_tensor("xrl", [128, 2, c.OBW], F32))
            otl = P5.enter_context(nc.sbuf_tensor("otl", [128, 2, c.OBW], F32))
            psW = [P5.enter_context(nc.psum_tensor(f"psW{i}", [128, c.OBW], F32)) for i in range(2)]

            sb.barrier(p4_snap, exclude=("cc",))
            cc_tok = ("cc", sb.cnt["cc"])

            def cco_src(ch, tgf):
                half, hl = divmod(ch, c.QH)
                g2, hr = divmod(hl, c.GRP)
                row = (half * c.GRP + hr) * 128
                return cc_out.ap()[g2, row:row + 128, tgf * c.TPF:(tgf + 1) * c.TPF]

            free_atm = None
            free_wo = [None, None]
            free_psW = [None, None]
            free_ftl2 = [None, None]
            free_xrl = [None, None]
            free_otl = [None, None]
            widx2 = 0
            t_ow = None
            t_wmm = None
            for tgf in range(c.NTGF):
                sb.waits("sync", [cc_tok, free_atm])
                atm_tok = []
                for ch in range(c.NWCH):
                    atm_tok.append(sb.dma("sync", atm[:, ch, :], cco_src(ch, tgf), "dam"))
                t_atm = atm_tok[-1]
                for ob in range(c.NOBW):
                    wsl = (tgf * c.NOBW + ob) % 2
                    sb.waits("gpsimd", [t_wocast, free_wo[wsl]])
                    t_wo = sb.dma("gpsimd",
                                  wob[:, wsl, :].rearrange("p (c f) -> p c f", c=c.NWCH),
                                  wo_bf.ap()[ob].rearrange("c p f -> p c f"), f"dwo{wsl}")
                    for tb in range(c.TPF // 128):
                        sl = widx2 % 2
                        gtb = tgf * (c.TPF // 128) + tb
                        sb.waits("scalar", ffn_w_all if widx2 == 0 else [])
                        sb.waits("scalar", [free_ftl2[sl]])
                        srcf = ffn_scr.ap()[gtb * 128:(gtb + 1) * 128,
                                            ob * c.OBW:(ob + 1) * c.OBW]
                        t_lf = sb.dma("scalar", ftl2[:, sl, :], srcf, f"dfl{sl}")
                        sb.waits("scalar", [free_xrl[sl]])
                        srcx = x_oc[gtb * 128:(gtb + 1) * 128, ob * c.OBW:(ob + 1) * c.OBW]
                        t_lx = sb.dma("scalar", xrl[:, sl, :], srcx, f"dxr{sl}")
                        sb.waits("tensor", [t_wo, t_atm, free_psW[sl]])
                        for ch in range(c.NWCH):
                            t_wmm = sb.op("tensor", lambda E, wsl=wsl, sl=sl, ch=ch, tb=tb: E.matmul(
                                psW[sl][:, :], atm[:, ch, tb * 128:(tb + 1) * 128],
                                wob[:, wsl, ch * c.OBW:(ch + 1) * c.OBW],
                                start=(ch == 0), stop=(ch == c.NWCH - 1)),
                                inc="pe" if ch == c.NWCH - 1 else None)
                        sb.waits("vector", [t_wmm, t_lf, free_otl[sl]])
                        t_a1 = sb.op("vector", lambda E, sl=sl: E.tensor_tensor(
                            otl[:, sl, :], psW[sl][:, :], ftl2[:, sl, :], op=ALU.add), inc="vv")
                        free_psW[sl] = t_a1
                        free_ftl2[sl] = t_a1
                        sb.waits("vector", [t_lx])
                        t_a2 = sb.op("vector", lambda E, sl=sl: E.tensor_tensor(
                            otl[:, sl, :], otl[:, sl, :], xrl[:, sl, :], op=ALU.add), inc="vv")
                        free_xrl[sl] = t_a2
                        sb.waits("sync", [t_a2])
                        dsto = out_ext[gtb * 128:(gtb + 1) * 128,
                                       ob * c.OBW:(ob + 1) * c.OBW]
                        t_ow = sb.dma("sync", dsto, otl[:, sl, :], f"dou{sl}")
                        free_otl[sl] = t_ow
                        widx2 += 1
                    free_wo[wsl] = t_wmm
                free_atm = t_wmm

            sb.waits("sync", [free_otl[0], free_otl[1]])

        with nc.Block() as block:
            @block.tensor
            def _(E):
                for f in sb.streams["tensor"]:
                    f(E)

            @block.vector
            def _(E):
                for f in sb.streams["vector"]:
                    f(E)

            @block.scalar
            def _(E):
                for f in sb.streams["scalar"]:
                    f(E)

            @block.sync
            def _(E):
                for f in sb.streams["sync"]:
                    f(E)

            @block.gpsimd
            def _(E):
                for f in sb.streams["gpsimd"]:
                    f(E)

    return nc



def _rope_tables(S, HD, theta):
    inv = 1.0 / theta ** (np.arange(0, HD, 2, dtype=np.float64) / HD)
    t = np.arange(S, dtype=np.float64)
    freqs = np.outer(t, inv)
    emb = np.concatenate([freqs, freqs], axis=-1)
    return np.cos(emb).T.astype(np.float32), np.sin(emb).T.astype(np.float32)


def make_core_inputs(c, x_b, Wq_h, Wk_h, Wv_h, Wo_h, WT_h, AT, BfT_h, bias_h,
                     coeffs, scales, x_oc):
    cos, sin = _rope_tables(c.S, c.HD, c.rope_theta)
    s = 1.0 / math.sqrt(c.HD)
    m0 = (np.arange(128)[:, None] <= np.arange(c.QT)[None, :]).astype(np.float32)
    m1 = ((np.arange(128)[:, None] + 128) <= np.arange(c.QT)[None, :]).astype(np.float32)
    m0 = np.ascontiguousarray(np.tile(m0, (1, c.GRP))).astype(BF)
    m1 = np.ascontiguousarray(np.tile(m1, (1, c.GRP))).astype(BF)
    R = np.zeros((128, 128), np.float32)
    hh = 64
    for d in range(hh):
        R[d, d + hh] = -1.0
    for d in range(hh, 128):
        R[d, d - hh] = 1.0
    rt = np.ascontiguousarray(R.T).astype(BF)

    return {
        "x_b": np.ascontiguousarray(x_b),
        "x_oc": np.ascontiguousarray(x_oc),
        "wq_t": np.ascontiguousarray(
            Wq_h.reshape(c.ND, 128, c.QH, 128).transpose(2, 0, 1, 3)),
        "wk_t": np.ascontiguousarray(
            Wk_h.reshape(c.ND, 128, c.KVH, 128).transpose(2, 0, 1, 3)),
        "wv_t": np.ascontiguousarray(Wv_h.reshape(c.ND, 128, c.KVH * 128)),
        "wo_t": np.ascontiguousarray(
            Wo_h.reshape(c.NWCH, 128, c.NOBW, c.OBW).transpose(2, 0, 1, 3)),
        "wt_t": np.ascontiguousarray(
            WT_h.reshape(c.ND, 128, c.NOBF, c.OBF).transpose(2, 0, 1, 3)),
        "at_t": np.ascontiguousarray(
            AT.reshape(c.ND, 128, c.NERB, 128).transpose(2, 0, 1, 3)),
        "bft_t": np.ascontiguousarray(
            BfT_h.reshape(c.NERB, 128, c.NOBF, c.OBF).transpose(2, 0, 1, 3)),
        "bias_in": np.ascontiguousarray(bias_h.reshape(1, c.MYO)),
        "cf_rep": np.ascontiguousarray(
            np.repeat(coeffs, c.ER // len(coeffs)).reshape(c.NERB, 128).T),
        "sc_rep": np.ascontiguousarray(
            np.repeat(scales, c.ER // len(scales)).reshape(c.NERB, 128).T),
        "cosq": np.ascontiguousarray(cos * s),
        "sinq": np.ascontiguousarray(sin * s),
        "cosk": np.ascontiguousarray(cos),
        "sink": np.ascontiguousarray(sin),
        "mask0": m0,
        "mask1": m1,
        "rt": rt,
    }


_NC_CACHE = {}


def kernel(L0_attn_hidden_states, L0_attn_Wq, L0_attn_Wk, L0_attn_Wv, L0_attn_Wo,
           L0_ffn_W, L0_ffn_bias, L0_ffn_A, L0_ffn_Bf, L0_ffn_coeffs, L0_ffn_scales):
    x = np.asarray(L0_attn_hidden_states, dtype=np.float32)
    Wq = np.asarray(L0_attn_Wq, np.float32)
    Wk = np.asarray(L0_attn_Wk, np.float32)
    Wv = np.asarray(L0_attn_Wv, np.float32)
    Wo = np.asarray(L0_attn_Wo, np.float32)
    W = np.asarray(L0_ffn_W, np.float32)
    bias = np.asarray(L0_ffn_bias, np.float32)
    A = np.asarray(L0_ffn_A, np.float32)
    Bf = np.asarray(L0_ffn_Bf, np.float32)
    coeffs = np.asarray(L0_ffn_coeffs, np.float32)
    scales = np.asarray(L0_ffn_scales, np.float32)

    B, S, D = x.shape
    c = Cfg(S=S, D=D)
    WT = np.ascontiguousarray(W.T)
    AT = np.ascontiguousarray(A.reshape(c.ER, D).T)
    BfT = np.ascontiguousarray(Bf.transpose(0, 2, 1).reshape(c.ER, D))

    key = (S, D)
    if key not in _NC_CACHE:
        _NC_CACHE[key] = build_nc(c)
    nc = _NC_CACHE[key]

    in_maps = []
    for core in range(8):
        b, h = core // 2, core % 2
        qs = c.QH * 128
        ks = c.KVH * 128
        o0 = h * c.MYO
        im = make_core_inputs(
            c, x[b],
            Wq[:, h * qs:(h + 1) * qs],
            Wk[:, h * ks:(h + 1) * ks],
            Wv[:, h * ks:(h + 1) * ks],
            Wo[:, o0:o0 + c.MYO],
            WT[:, o0:o0 + c.MYO],
            AT,
            BfT[:, o0:o0 + c.MYO],
            bias[o0:o0 + c.MYO],
            coeffs, scales,
            x[b][:, o0:o0 + c.MYO],
        )
        in_maps.append(im)

    res = run_bass_kernel_spmd(nc, in_maps, core_ids=list(range(8)))
    out = np.empty((B, S, D), np.float32)
    for core in range(8):
        b, h = core // 2, core % 2
        out[b, :, h * c.MYO:(h + 1) * c.MYO] = res.results[core]["out"]
    return out
